# version 17
# baseline (speedup 1.0000x reference)
import math
import contextlib
import numpy as np
import ml_dtypes
import concourse.bass as bass
import concourse.mybir as mybir
from concourse.bass_utils import run_bass_kernel_spmd

F32 = mybir.dt.float32
BF16 = mybir.dt.bfloat16
AX = mybir.AxisListType.X
ALU = mybir.AluOpType
ACT = mybir.ActivationFunctionType

T = 2048
D = 2048
NH = 16
DN = 128
DR = 64
DV = 128
QL = 1536
KL = 512
NE = 32
RK = 16
NTC = T // 128
SCALE = 1.0 / math.sqrt(DN + DR)
MASKNEG = -1.0e9

O_ID = 0
O_EP = 128
O_AW = 640
O_ATTRB = 656
O_WAB = 688
O_ONE = 704
O_MB = 720
O_LN1 = 1024
O_QLAT = 33792
O_WBIG = 58368
O_LN1T = 82944
O_QLT = 84992
O_TRB = 86528
O_KVLAT = 88576
NB = 96768
AT_KVBN = 1024
AT_KVBV = 9216
AT_OW = 17408
AT_QN = 50176
AT_QR = 54272
AT_KNH = 58368
AT_KRH = 60416
AT_VH = 64512
AT_PB = 66624
AT_ATS2 = 70720
AT_ABR = 72768
AT_ABF = 76864
C_UA = 1024
C_VA = 9216
C_WR = 17408
C_LN2T = 17920
C_LN2F = 19968
C_ZSB = 22016
C_ZF = 22528
C_GF = 23040
C_G = 23168
C_GE = 23296

F_TRILQ = 0
F_QLN = 128
F_KVLN = 140
F_ST = 144
F_ESB = F_ST + 48
F_MASKF = F_ST + 80
F_SC2 = F_ST + 112
F_XB = 304
F_X2S = 2352
NF = 4400

(S_SSUM, S_M, S_S2, S_VARR, S_RSTD, S_MX, S_NEGMS, S_Z, S_LNZ, S_B2,
 S_M1, S_NEGM1, S_ZR, S_M2, S_S2G, S_DEN, S_REC, S_F, S_SA, S_SB, S_SC,
 S_T0, S_T1) = range(23)

ENGINES = ("sync", "gpsimd", "tensor", "vector", "scalar")


class Prog:
    def __init__(self, nc, es):
        self.nc = nc
        self.sched = {e: [] for e in ENGINES}
        self.semh = {}
        self.cnt = {}
        self.dma_sems = ("dc", "dws", "dwa", "dwo", "dxx", "dkr", "dqs0", "dqs1",
                         "dp0", "dp1", "dat", "db0", "db1", "dxw", "dc0c", "dc1c",
                         "dout", "dtap")
        for s in ("tn", "vc", "sa") + self.dma_sems:
            self.semh[s] = es.enter_context(nc.semaphore(s))
            self.cnt[s] = 0
        self.bank_rd_vc = [0] * 8
        self.bank_rd_sa = [0] * 8

    def op(self, eng, name, *args, inc=None, **kw):
        self.sched[eng].append([name, args, kw, inc])

    def wait(self, eng, sem, val):
        if val <= 0:
            return
        self.sched[eng].append(["__wait__", (sem, val), {}, None])

    def bump(self, sem, n):
        self.cnt[sem] += n
        return self.cnt[sem]

    def dma(self, eng, dst, src, sem):
        v = self.bump(sem, 16)
        self.op(eng, "dma_start", dst, src, inc=(sem, 16))
        return v

    def _pub(self, eng, sem):
        v = self.bump(sem, 1)
        last = self.sched[eng][-1] if self.sched[eng] else None
        if last is not None and last[0] != "__wait__" and last[3] is None:
            last[3] = (sem, 1)
        else:
            self.op(eng, "drain", inc=(sem, 1))
        return v

    def vpub(self):
        return self._pub("vector", "vc")

    def vdrain(self):
        self.op("vector", "drain")

    def spub(self):
        return self._pub("scalar", "sa")

    def tpub(self):
        return self._pub("tensor", "tn")

    def replay(self, eng, e):
        semh = self.semh
        for name, args, kw, inc in self.sched[eng]:
            if name == "__wait__":
                e.wait_ge(semh[args[0]], args[1])
                continue
            ins = getattr(e, name)(*args, **kw)
            if inc is not None:
                ins.then_inc(semh[inc[0]], inc[1])

    def barrier(self, engines=("tensor", "vector", "scalar")):
        for eng in engines:
            for sem in ("tn", "vc", "sa") + self.dma_sems:
                self.wait(eng, sem, self.cnt[sem])

    def guard_banks(self, banks):
        v = max(self.bank_rd_vc[b] for b in banks)
        s = max(self.bank_rd_sa[b] for b in banks)
        self.wait("tensor", "vc", v)
        self.wait("tensor", "sa", s)


def strided_cols(arena, base, n, stride, w, limit=NB):
    if base + n * stride <= limit:
        big = arena[:, base: base + n * stride]
        return big.rearrange("p (f t) -> p f t", t=stride)[:, :, 0:w]
    pad = stride - w
    big = arena[:, base - pad: base - pad + n * stride]
    return big.rearrange("p (f t) -> p f t", t=stride)[:, :, pad:stride]


def build(taps=()):
    nc = bass.Bass("TRN2", target_bir_lowering=False, debug=False)

    x = nc.dram_tensor("x", [T, D], F32, kind="ExternalInput").ap()
    qa = nc.dram_tensor("qa", [D, QL], F32, kind="ExternalInput").ap()
    kva = nc.dram_tensor("kva", [D, QL], F32, kind="ExternalInput").ap()
    qb = nc.dram_tensor("qb", [QL, NH * (DN + DR)], F32, kind="ExternalInput").ap()
    kvbn = nc.dram_tensor("kvbn", [KL, NH * DN], F32, kind="ExternalInput").ap()
    kvbv = nc.dram_tensor("kvbv", [KL, NH * DV], F32, kind="ExternalInput").ap()
    ow = nc.dram_tensor("ow", [NH * DV, D], F32, kind="ExternalInput").ap()
    wrt = nc.dram_tensor("wrt", [D, NE], F32, kind="ExternalInput").ap()
    ua = nc.dram_tensor("ua", [D, NE * RK], F32, kind="ExternalInput").ap()
    va = nc.dram_tensor("va", [NE * RK, D], F32, kind="ExternalInput").ap()
    attrt = nc.dram_tensor("attrt", [NE, NE], F32, kind="ExternalInput").ap()
    wac = nc.dram_tensor("wac", [NE, 1], F32, kind="ExternalInput").ap()
    qln = nc.dram_tensor("qln", [128, 12], F32, kind="ExternalInput").ap()
    kvln = nc.dram_tensor("kvln", [128, 4], F32, kind="ExternalInput").ap()
    out = nc.dram_tensor("out", [T, D], F32, kind="ExternalOutput").ap()

    q_spill = nc.dram_tensor("q_spill", [NH * (DN + DR), T], BF16, kind="Internal").ap()
    kr_spill = nc.dram_tensor("kr_spill", [NH * DR, T], BF16, kind="Internal").ap()
    attn_spill = nc.dram_tensor("attn_spill", [T, NH * DV], BF16, kind="Internal").ap()
    x2_spill = nc.dram_tensor("x2_spill", [T, D], F32, kind="Internal").ap()

    ident_np = np.eye(128, dtype=ml_dtypes.bfloat16)
    ident_h = nc.inline_tensor(ident_np, "identc").ap()
    trilq_np = np.where(np.arange(128)[None, :] >= np.arange(128)[:, None],
                        np.float32(0.0), np.float32(MASKNEG)).astype(np.float32)
    trilq_h = nc.inline_tensor(trilq_np, "trilqc").ap()
    ep_np = np.zeros((NE, NE * RK), dtype=ml_dtypes.bfloat16)
    for s in range(NE):
        ep_np[s, s * RK:(s + 1) * RK] = 1.0
    ep_h = nc.inline_tensor(ep_np, "epc").ap()

    tap_out = {}
    tap_shapes = {"ln1": ([128, 32768], BF16), "qlat": ([128, 24576], BF16),
                  "kvlat": ([128, 8192], BF16), "q_spill": ([NH * (DN + DR), T], BF16),
                  "kr": ([NH * DR, T], BF16), "attn": ([T, NH * DV], BF16),
                  "x2": ([T, D], F32)}
    for t_ in taps:
        sh, dt = tap_shapes[t_]
        tap_out[t_] = nc.dram_tensor("tap_" + t_, sh, dt, kind="ExternalOutput").ap()

    with contextlib.ExitStack() as es:
        AB = es.enter_context(nc.sbuf_tensor("AB", [128, NB], BF16))
        AF = es.enter_context(nc.sbuf_tensor("AF", [128, NF], F32))
        PS = es.enter_context(nc.psum_tensor("PS", [128, 4096], F32))
        p = Prog(nc, es)

        def B(off, w, p0=0, p1=128):
            return AB[p0:p1, off:off + w]

        def F(off, w, p0=0, p1=128):
            return AF[p0:p1, off:off + w]

        def PB(b, o=0, w=512, p0=0, p1=128):
            return PS[p0:p1, 512 * b + o: 512 * b + o + w]

        def ST(i):
            return AF[:, F_ST + i: F_ST + i + 1]

        p.dma("sync", B(O_ID, 128), ident_h, "dc")
        p.dma("sync", F(F_TRILQ, 128), trilq_h, "dc")
        p.dma("sync", B(O_EP, NE * RK, 0, NE), ep_h, "dc")
        p.dma("sync", F(F_QLN, 12), qln, "dc")
        d_consts = p.dma("sync", F(F_KVLN, 4), kvln, "dc")
        p.dma("gpsimd", B(O_ATTRB, NE, 0, NE), attrt, "dws")
        d_wab = p.dma("gpsimd", B(O_WAB, 1, 0, NE), wac, "dws")
        for k in range(16):
            p.dma("gpsimd", B(O_WBIG + 1536 * k, 1536), qa[128 * k:128 * (k + 1), :],
                  "dwa")
        d_qa = p.cnt["dwa"]
        dx0 = p.dma("sync", F(F_XB, 2048), x[0:128, :], "dxx")

        p.wait("tensor", "dws", d_wab)
        p.op("tensor", "matmul", PB(7, 0, 1, 0, NE), B(O_ATTRB, NE, 0, NE),
             B(O_WAB, 1, 0, NE), start=True, stop=True)
        t_aw = p.tpub()
        p.wait("scalar", "tn", t_aw)
        p.op("scalar", "activation", B(O_AW, 1, 0, NE), PB(7, 0, 1, 0, NE), ACT.Sigmoid,
             bias=0.0, scale=1.0)
        p.bank_rd_sa[7] = p.spub()

        def ln_chain(src_f32, dst_bf, eps):
            p.op("vector", "reduce_sum", ST(S_SSUM), src_f32, axis=AX)
            p.vdrain()
            p.op("vector", "tensor_scalar_mul", ST(S_M), ST(S_SSUM), 1.0 / D)
            p.vdrain()
            p.op("vector", "scalar_tensor_tensor", B(O_TRB, 2048), src_f32, ST(S_M),
                 src_f32, ALU.subtract, ALU.mult, accum_out=ST(S_S2))
            p.vdrain()
            p.op("vector", "tensor_scalar", ST(S_VARR), ST(S_S2), 1.0 / D, eps,
                 ALU.mult, ALU.add)
            v = p.vpub()
            p.wait("scalar", "vc", v)
            p.op("scalar", "activation", ST(S_T0), ST(S_VARR), ACT.Ln, bias=0.0, scale=1.0)
            sv = p.spub()
            p.wait("scalar", "sa", sv)
            p.op("scalar", "activation", ST(S_RSTD), ST(S_T0), ACT.Exp, bias=0.0, scale=-0.5)
            s = p.spub()
            p.wait("vector", "sa", s)
            p.op("vector", "tensor_scalar", dst_bf, src_f32, ST(S_M), ST(S_RSTD),
                 ALU.subtract, ALU.mult)
            return p.vpub()

        def transpose_tiles(src_off, ntile, dst_ap_fn, bA=3, bB=7):
            vlast = 0
            for i0 in range(0, ntile, 4):
                cnt = min(4, ntile - i0)
                bk = bA if (i0 // 4) % 2 == 0 else bB
                p.guard_banks([bk])
                for i in range(i0, i0 + cnt):
                    p.op("tensor", "matmul", PB(bk, 128 * (i - i0), 128),
                         B(src_off + 128 * i, 128), B(O_ID, 128), start=True, stop=True,
                         skip_group_check=True)
                tv = p.tpub()
                p.wait("vector", "tn", tv)
                p.op("vector", "tensor_copy", dst_ap_fn(i0, cnt), PB(bk, 0, 128 * cnt))
                vlast = p.vpub()
                p.bank_rd_vc[bk] = vlast
            return vlast

        def rms_chain(src_f32, n, dst_bf, eps):
            p.op("vector", "scalar_tensor_tensor", B(O_TRB, n), src_f32,
                 1.0, src_f32, ALU.mult, ALU.mult, accum_out=ST(S_SB))
            p.vdrain()
            p.op("vector", "tensor_scalar", ST(S_SC), ST(S_SB), 1.0 / n, eps,
                 ALU.mult, ALU.add)
            v = p.vpub()
            p.wait("scalar", "vc", v)
            p.op("scalar", "activation", ST(S_T1), ST(S_SC), ACT.Ln, bias=0.0, scale=1.0)
            sv = p.spub()
            p.wait("scalar", "sa", sv)
            p.op("scalar", "activation", ST(S_LNZ), ST(S_T1), ACT.Exp, bias=0.0, scale=-0.5)
            s = p.spub()
            p.wait("vector", "sa", s)
            p.op("vector", "tensor_scalar_mul", dst_bf, src_f32, ST(S_LNZ))
            return p.vpub()

        ln1_ready = [0] * NTC
        v_ln = [0] * NTC
        t_lntr = [0] * NTC
        dx_x = [0] * (NTC + 1)
        dx_x[0] = dx0
        p.wait("vector", "dc", d_consts)
        p.wait("vector", "dxx", dx_x[0])
        v_ln[0] = ln_chain(F(F_XB, 2048), B(O_LN1T, 2048), 1e-5)

        for c in range(NTC):
            def ln1_dst(i0, cnt, c=c):
                return strided_cols(AB, O_LN1 + 2048 * i0 + 128 * c, cnt, 2048, 128)
            p.wait("tensor", "vc", v_ln[c])
            ln1_ready[c] = transpose_tiles(O_LN1T, 16, ln1_dst)
            t_lntr[c] = p.cnt["tn"]
            p.wait("tensor", "dwa", d_qa)
            p.wait("tensor", "vc", ln1_ready[c])
            p.guard_banks([0, 1, 2])
            for k in range(16):
                lhs = B(O_LN1 + 2048 * k + 128 * c, 128)
                for b in range(3):
                    p.op("tensor", "matmul", PB(b), lhs,
                         B(O_WBIG + 1536 * k + 512 * b, 512),
                         start=(k == 0), stop=(k == 15), skip_group_check=True)
            t_qraw = p.tpub()
            if c + 1 < NTC:
                p.wait("sync", "vc", v_ln[c])
                dx_x[c + 1] = p.dma("sync", F(F_XB, 2048),
                                    x[128 * (c + 1):128 * (c + 2), :], "dxx")
                p.wait("vector", "dxx", dx_x[c + 1])
                p.wait("vector", "tn", t_lntr[c])
                v_ln[c + 1] = ln_chain(F(F_XB, 2048), B(O_LN1T, 2048), 1e-5)
            p.wait("vector", "tn", t_qraw)
            p.op("vector", "tensor_copy", AF[:, F_X2S:F_X2S + QL], PS[:, 0:QL])
            v_ev = p.vpub()
            for b in range(3):
                p.bank_rd_vc[b] = v_ev
            p.vdrain()
            rms_chain(AF[:, F_X2S:F_X2S + QL], QL, B(O_QLT, QL), 1e-6)
            v_qlt = p.cnt["vc"]

            def qlat_dst(i0, cnt, c=c):
                return strided_cols(AB, O_QLAT + 2048 * i0 + 128 * c, cnt, 2048, 128)
            p.wait("tensor", "vc", v_qlt)
            transpose_tiles(O_QLT, 12, qlat_dst)

        if "ln1" in taps:
            p.wait("sync", "vc", p.cnt["vc"])
            p.dma("sync", tap_out["ln1"], AB[:, O_LN1:O_LN1 + 32768], "dtap")
        if "qlat" in taps:
            p.wait("sync", "vc", p.cnt["vc"])
            p.dma("sync", tap_out["qlat"], AB[:, O_QLAT:O_QLAT + 24576], "dtap")

        p.wait("gpsimd", "tn", p.cnt["tn"])
        for k in range(16):
            p.dma("gpsimd", B(O_WBIG + 1536 * k, 1536), kva[128 * k:128 * (k + 1), :],
                  "dwa")
        d_kva = p.cnt["dwa"]

        kr_dma = 0

        def emit_kvmm(c):
            p.wait("tensor", "dwa", d_kva)
            p.guard_banks([0, 1, 2])
            for k in range(16):
                lhs = B(O_LN1 + 2048 * k + 128 * c, 128)
                for b in range(3):
                    p.op("tensor", "matmul", PB(b), lhs,
                         B(O_WBIG + 1536 * k + 512 * b, 512),
                         start=(k == 0), stop=(k == 15), skip_group_check=True)
            return p.tpub()

        t_kv = emit_kvmm(0)
        for c in range(NTC):
            p.wait("vector", "tn", t_kv)
            p.op("vector", "tensor_copy", B(O_TRB, 512), PB(1))
            p.op("vector", "tensor_copy", B(O_TRB + 512, 512), PB(2))
            p.op("vector", "tensor_copy", AF[:, F_X2S:F_X2S + KL], PB(0))
            v_ev = p.vpub()
            for b in range(3):
                p.bank_rd_vc[b] = v_ev
            if c + 1 < NTC:
                t_kv = emit_kvmm(c + 1)
            def kr_dst(i0, cnt):
                return B(O_LN1T + 128 * i0, 128 * cnt)
            p.wait("vector", "dkr", kr_dma)
            p.wait("tensor", "vc", v_ev)
            v_kr = transpose_tiles(O_TRB, 8, kr_dst)
            p.wait("sync", "vc", v_kr)
            src = B(O_LN1T, 1024).rearrange("p (f t) -> p f t", t=128)
            dst = kr_spill[0:1024, 128 * c:128 * (c + 1)].rearrange(
                "(f p) t -> p f t", p=128)
            kr_dma = p.dma("sync", dst, src, "dkr")
            p.vdrain()
            rms_chain(AF[:, F_X2S:F_X2S + KL], KL, B(O_QLT, 512), 1e-6)
            v_klt = p.cnt["vc"]

            def kvlat_dst(i0, cnt, c=c):
                return strided_cols(AB, O_KVLAT + 2048 * i0 + 128 * c, cnt, 2048, 128)
            p.wait("tensor", "vc", v_klt)
            transpose_tiles(O_QLT, 4, kvlat_dst)

        if "kvlat" in taps:
            p.wait("sync", "vc", p.cnt["vc"])
            p.dma("sync", tap_out["kvlat"], AB[:, O_KVLAT:O_KVLAT + 8192], "dtap")
        if "kr" in taps:
            p.wait("sync", "dkr", p.cnt["dkr"])
            p.dma("sync", tap_out["kr"], kr_spill, "dtap")

        p.wait("gpsimd", "tn", p.cnt["tn"])

        def qb_off(k):
            return (O_WBIG + 3072 * k) if k < 8 else (O_LN1 + 3072 * (k - 8))
        for k in range(12):
            p.dma("gpsimd", B(qb_off(k), 3072), qb[128 * k:128 * (k + 1), :], "dwa")
        p.wait("vector", "dwa", p.cnt["dwa"])
        for k in range(12):
            p.op("vector", "tensor_scalar_mul", B(qb_off(k), 3072), B(qb_off(k), 3072),
                 AF[:, F_QLN + k:F_QLN + k + 1])
        v_qb = p.vpub()
        p.wait("tensor", "vc", v_qb)

        stg = [O_LN1T, O_TRB]
        stg_free = [0, 0]
        p.wait("vector", "dkr", p.cnt["dkr"])
        for j in range(24):
            sb = j % 2
            p.wait("vector", "dqs0" if sb == 0 else "dqs1", stg_free[sb])
            for g in range(4):
                bk = g
                p.guard_banks([bk])
                for k in range(12):
                    p.op("tensor", "matmul", PB(bk), B(qb_off(k) + 128 * j, 128),
                         B(O_QLAT + 2048 * k + 512 * g, 512),
                         start=(k == 0), stop=(k == 11), skip_group_check=True)
                tv = p.tpub()
                p.wait("vector", "tn", tv)
                p.op("vector", "tensor_copy", B(stg[sb] + 512 * g, 512), PB(bk))
                p.bank_rd_vc[bk] = p.vpub()
            p.wait("sync", "vc", p.bank_rd_vc[3])
            stg_free[sb] = p.dma("sync", q_spill[128 * j:128 * (j + 1), :],
                                 B(stg[sb], 2048), "dqs0" if sb == 0 else "dqs1")

        if "q_spill" in taps:
            p.wait("sync", "dqs0", p.cnt["dqs0"])
            p.wait("sync", "dqs1", p.cnt["dqs1"])
            p.dma("sync", tap_out["q_spill"], q_spill, "dtap")

        p.wait("gpsimd", "tn", p.cnt["tn"])

        for k in range(4):
            p.dma("gpsimd", B(AT_KVBN + 2048 * k, 2048),
                  kvbn[128 * k:128 * (k + 1), :], "dwa")
        for k in range(4):
            p.dma("gpsimd", B(AT_KVBV + 2048 * k, 2048),
                  kvbv[128 * k:128 * (k + 1), :], "dwa")
        p.wait("vector", "dwa", p.cnt["dwa"])
        for k in range(4):
            p.op("vector", "tensor_scalar_mul", B(AT_KVBN + 2048 * k, 2048),
                 B(AT_KVBN + 2048 * k, 2048), AF[:, F_KVLN + k:F_KVLN + k + 1])
        for k in range(4):
            p.op("vector", "tensor_scalar_mul", B(AT_KVBV + 2048 * k, 2048),
                 B(AT_KVBV + 2048 * k, 2048), AF[:, F_KVLN + k:F_KVLN + k + 1])
        p.op("vector", "memset", B(O_ONE, 1), 1.0)
        p.op("vector", "memset", B(O_MB + 8, 128, 0, 1), 1.0)
        p.op("vector", "memset", strided_cols(AB, AT_VH + 128, 16, 132, 1), 1.0)
        v_kvb = p.vpub()
        d_ow = 0
        for k in range(16):
            d_ow = p.dma("gpsimd", B(AT_OW + 2048 * k, 2048),
                         ow[128 * k:128 * (k + 1), :], "dwo")

        dq = [0, 0]
        p.wait("sync", "dqs0", p.cnt["dqs0"])
        p.wait("sync", "dqs1", p.cnt["dqs1"])
        p.wait("sync", "dkr", p.cnt["dkr"])
        p.dma("sync", B(AT_QN, 2048), q_spill[0:128, :], "dp0")
        p.dma("sync", B(AT_QR, 2048, 0, 64), q_spill[128:192, :], "dp0")
        dq[0] = p.dma("sync", B(AT_KRH, 2048, 0, 64), kr_spill[0:64, :], "dp0")

        p.wait("tensor", "vc", v_kvb)
        att_dma = 0
        ebuf_free = [0, 0]
        for h in range(NH):
            bufq = h % 2
            qn = AT_QN + 2048 * bufq
            qr = AT_QR + 2048 * bufq
            krh = AT_KRH + 2048 * bufq
            p.wait("tensor", "dp0" if bufq == 0 else "dp1", dq[bufq])
            p.guard_banks(list(range(8)))
            p.guard_banks([0, 1, 2, 3])
            for n in range(4):
                for kk in range(4):
                    p.op("tensor", "matmul", PB(n), B(AT_KVBN + 2048 * kk + 128 * h, 128),
                         B(O_KVLAT + 2048 * kk + 512 * n, 512),
                         start=(kk == 0), stop=(kk == 3), skip_group_check=True)
            t_kn = p.tpub()
            p.wait("vector", "tn", t_kn)
            for n in range(4):
                p.op("vector", "tensor_copy", B(AT_KNH + 512 * n, 512), PB(n))
            v_kn = p.vpub()
            for n in range(4):
                p.bank_rd_vc[n] = v_kn
            if h + 1 < NH:
                hn = h + 1
                nb = hn % 2
                p.wait("sync", "tn", p.cnt["tn"])
                psem = "dp0" if nb == 0 else "dp1"
                p.dma("sync", B(AT_QN + 2048 * nb, 2048),
                      q_spill[192 * hn:192 * hn + 128, :], psem)
                p.dma("sync", B(AT_QR + 2048 * nb, 2048, 0, 64),
                      q_spill[192 * hn + 128:192 * hn + 192, :], psem)
                dq[nb] = p.dma("sync", B(AT_KRH + 2048 * nb, 2048, 0, 64),
                               kr_spill[64 * hn:64 * hn + 64, :], psem)
            sqb = AT_PB
            p.wait("vector", "dp0" if bufq == 0 else "dp1", dq[bufq])
            p.wait("vector", "tn", max(ebuf_free))
            p.op("vector", "tensor_tensor", B(sqb, 2048), B(qn, 2048), B(qn, 2048),
                 ALU.mult)
            v1 = p.vpub()
            p.wait("tensor", "vc", max(v1, v_kn))
            p.guard_banks([0, 1, 2, 3])
            for n in range(4):
                p.op("tensor", "matmul", PS[0:1, 512 * n:512 * n + 512],
                     B(O_ONE, 1), B(sqb + 512 * n, 512),
                     start=True, stop=False, skip_group_check=True)
            t_qa = p.tpub()
            p.wait("vector", "tn", t_qa)
            p.op("vector", "tensor_tensor", B(sqb, 2048, 0, 64), B(qr, 2048, 0, 64),
                 B(qr, 2048, 0, 64), ALU.mult)
            v2 = p.vpub()
            p.wait("tensor", "vc", v2)
            for n in range(4):
                p.op("tensor", "matmul", PS[0:1, 512 * n:512 * n + 512],
                     B(O_ONE, 1, 0, 64), B(sqb + 512 * n, 512, 0, 64),
                     start=False, stop=True, skip_group_check=True)
            t_q2 = p.tpub()
            p.wait("vector", "tn", t_q2)
            p.op("vector", "tensor_reduce", AF[0:1, F_ST + S_MX:F_ST + S_MX + 1],
                 PS[0:1, 0:2048], AX, ALU.max)
            v_qm = p.vpub()
            for n in range(4):
                p.bank_rd_vc[n] = v_qm
            p.vdrain()
            p.op("vector", "tensor_tensor", B(sqb, 2048), B(AT_KNH, 2048),
                 B(AT_KNH, 2048), ALU.mult)
            v3 = p.vpub()
            p.wait("tensor", "vc", v3)
            p.guard_banks([0, 1, 2, 3])
            for n in range(4):
                p.op("tensor", "matmul", PS[0:1, 512 * n:512 * n + 512],
                     B(O_ONE, 1), B(sqb + 512 * n, 512),
                     start=True, stop=False, skip_group_check=True)
            t_ka = p.tpub()
            p.wait("vector", "tn", t_ka)
            p.op("vector", "tensor_tensor", B(sqb, 2048, 0, 64), B(krh, 2048, 0, 64),
                 B(krh, 2048, 0, 64), ALU.mult)
            v4 = p.vpub()
            p.wait("tensor", "vc", v4)
            for n in range(4):
                p.op("tensor", "matmul", PS[0:1, 512 * n:512 * n + 512],
                     B(O_ONE, 1, 0, 64), B(sqb + 512 * n, 512, 0, 64),
                     start=False, stop=True, skip_group_check=True)
            t_k2 = p.tpub()
            ebuf_free[0] = max(ebuf_free[0], t_k2)
            v_vh = 0
            for tg in range(4):
                bk = 4 + (tg % 2)
                p.guard_banks([bk])
                for tc in range(4):
                    tt = 4 * tg + tc
                    for kk in range(4):
                        p.op("tensor", "matmul", PB(bk, 128 * tc, 128),
                             B(O_KVLAT + 2048 * kk + 128 * tt, 128),
                             B(AT_KVBV + 2048 * kk + 128 * h, 128),
                             start=(tc == 0 and kk == 0), stop=(tc == 3 and kk == 3),
                             skip_group_check=True)
                tv = p.tpub()
                p.wait("vector", "tn", tv)
                p.op("vector", "tensor_copy",
                     strided_cols(AB, AT_VH + 132 * 4 * tg, 4, 132, 128),
                     PB(bk, 0, 512))
                v_vh = p.vpub()
                p.bank_rd_vc[bk] = v_vh
            p.wait("vector", "tn", t_k2)
            p.op("vector", "tensor_reduce", AF[0:1, F_ST + S_Z:F_ST + S_Z + 1],
                 PS[0:1, 0:2048], AX, ALU.max)
            v_km = p.vpub()
            for n in range(4):
                p.bank_rd_vc[n] = v_km
            p.wait("scalar", "vc", v_km)
            p.op("scalar", "activation", AF[0:1, F_ST + S_T0:F_ST + S_T0 + 1],
                 AF[0:1, F_ST + S_MX:F_ST + S_MX + 1], ACT.Ln, bias=0.0, scale=1.0)
            p.op("scalar", "activation", AF[0:1, F_ST + S_T1:F_ST + S_T1 + 1],
                 AF[0:1, F_ST + S_Z:F_ST + S_Z + 1], ACT.Ln, bias=0.0, scale=1.0)
            s_ln = p.spub()
            p.wait("vector", "sa", s_ln)
            p.op("vector", "tensor_add", AF[0:1, F_ST + S_B2:F_ST + S_B2 + 1],
                 AF[0:1, F_ST + S_T0:F_ST + S_T0 + 1],
                 AF[0:1, F_ST + S_T1:F_ST + S_T1 + 1])
            v_b = p.vpub()
            p.wait("scalar", "vc", v_b)
            p.op("scalar", "activation", AF[0:1, F_ST + S_REC:F_ST + S_REC + 1],
                 AF[0:1, F_ST + S_B2:F_ST + S_B2 + 1], ACT.Exp, bias=0.0, scale=0.5)
            s_mh = p.spub()
            p.wait("vector", "sa", s_mh)
            p.op("vector", "tensor_scalar_mul", B(O_MB, 1, 0, 1),
                 AF[0:1, F_ST + S_REC:F_ST + S_REC + 1], -SCALE)
            v_mb = p.vpub()
            p.wait("tensor", "vc", v_mb)
            p.guard_banks([0])
            p.op("tensor", "matmul", PB(0, 0, 1), B(O_MB + 8, 128, 0, 1),
                 B(O_MB, 1, 0, 1), start=True, stop=True, skip_group_check=True)
            t_bc = p.tpub()
            p.wait("vector", "tn", t_bc)
            p.op("vector", "tensor_copy", ST(S_NEGMS), PB(0, 0, 1))
            v_bias = p.vpub()
            p.bank_rd_vc[0] = v_bias
            p.wait("scalar", "vc", v_bias)
            def emit_pv(j, eb, qg0, s_exp):
                for i in range(j, NTC):
                    if i == j or (i % 4 == 0 and i // 4 > qg0):
                        p.wait("tensor", "sa", s_exp[i // 4])
                    bkk = 2 + i // 3
                    off = 129 * (i % 3)
                    p.op("tensor", "matmul",
                         PS[:, 512 * bkk + off: 512 * bkk + off + 129],
                         B(eb + 128 * i, 128), B(AT_VH + 132 * j, 129),
                         start=(j == 0 and i % 3 == 0), stop=False,
                         skip_group_check=True)
                ebuf_free[j % 2] = p.tpub()

            p.wait("tensor", "vc", v_vh)
            pv_args = None
            for j in range(NTC):
                eb = AT_PB + 2048 * (j % 2)
                qg0 = (128 * j) // 512
                s_exp = [0] * 4
                for qg in range(qg0, 4):
                    bk = qg % 2
                    p.guard_banks([bk])
                    p.op("tensor", "matmul", PB(bk), B(AT_KNH + 128 * j, 128),
                         B(qn + 512 * qg, 512), start=True, stop=False,
                         skip_group_check=True)
                    p.op("tensor", "matmul", PB(bk), B(krh + 128 * j, 128, 0, 64),
                         B(qr + 512 * qg, 512, 0, 64), start=False, stop=True,
                         skip_group_check=True)
                    t_st = p.tpub()
                    if qg == qg0:
                        p.wait("vector", "tn", t_st)
                        dg = PB(bk, 128 * (j % 4), 128)
                        p.op("vector", "tensor_add", dg, dg, F(F_TRILQ, 128))
                        v_mask = p.vpub()
                        p.bank_rd_vc[bk] = v_mask
                        p.wait("scalar", "vc", v_mask)
                    p.wait("scalar", "tn", max(t_st, ebuf_free[j % 2]))
                    p.op("scalar", "activation", B(eb + 512 * qg, 512), PB(bk),
                         ACT.Exp, bias=ST(S_NEGMS), scale=SCALE)
                    s_exp[qg] = p.spub()
                    p.bank_rd_sa[bk] = s_exp[qg]
                if pv_args is not None:
                    emit_pv(*pv_args)
                pv_args = (j, eb, qg0, list(s_exp))
            emit_pv(*pv_args)
            t_pv = ebuf_free[(NTC - 1) % 2]
            p.wait("vector", "tn", t_pv)
            p.wait("vector", "dat", att_dma)
            for i in range(NTC):
                bkk = 2 + i // 3
                off = 129 * (i % 3)
                p.op("vector", "reciprocal", ST(S_REC),
                     PS[:, 512 * bkk + off + 128: 512 * bkk + off + 129])
                p.vdrain()
                p.op("vector", "tensor_scalar_mul", B(AT_ATS2 + 128 * i, 128),
                     PS[:, 512 * bkk + off: 512 * bkk + off + 128], ST(S_REC))
                p.vdrain()
            v_at = p.vpub()
            for bq in range(2, 8):
                p.bank_rd_vc[bq] = v_at
            p.wait("sync", "vc", v_at)
            dst = attn_spill[0:T, 128 * h:128 * (h + 1)].rearrange(
                "(i p) v -> p i v", p=128)
            src2 = B(AT_ATS2, 2048).rearrange("p (i v) -> p i v", v=128)
            att_dma = p.dma("sync", dst, src2, "dat")

        if "attn" in taps:
            p.wait("sync", "dat", p.cnt["dat"])
            p.dma("sync", tap_out["attn"], attn_spill, "dtap")

        p.wait("tensor", "dwo", d_ow)
        p.guard_banks(list(range(8)))
        dab = [0, 0]
        p.wait("sync", "dat", att_dma)
        p.dma("sync", B(AT_ABR, 2048), attn_spill[0:128, :], "db0")
        dab[0] = p.dma("sync", F(F_XB, 2048), x[0:128, :], "db0")
        v_add = 0
        d_x2 = 0
        for c in range(NTC):
            sb = c % 2
            p.wait("tensor", "db0" if sb == 0 else "db1", dab[sb])

            def abf_dst(i0, cnt, sb=sb):
                return B(AT_ABF + 2048 * sb + 128 * i0, 128 * cnt)
            v_abf = transpose_tiles(AT_ABR + 2048 * sb, 16, abf_dst, bA=5, bB=7)
            p.wait("tensor", "vc", v_abf)
            p.guard_banks([0, 1, 2, 3])
            for n in range(4):
                for k in range(16):
                    p.op("tensor", "matmul", PB(n),
                         B(AT_ABF + 2048 * sb + 128 * k, 128),
                         B(AT_OW + 2048 * k + 512 * n, 512),
                         start=(k == 0), stop=(k == 15), skip_group_check=True)
            t_o = p.tpub()
            if c + 1 < NTC:
                nb = (c + 1) % 2
                p.wait("sync", "tn", t_o)
                p.dma("sync", B(AT_ABR + 2048 * nb, 2048),
                      attn_spill[128 * (c + 1):128 * (c + 2), :],
                      "db0" if nb == 0 else "db1")
            p.wait("vector", "tn", t_o)
            p.wait("vector", "dxw", d_x2)
            for n in range(4):
                p.op("vector", "tensor_add", AF[:, F_X2S + 512 * n:F_X2S + 512 * n + 512],
                     PB(n), AF[:, F_XB + 512 * n:F_XB + 512 * n + 512])
            v_add = p.vpub()
            for n in range(4):
                p.bank_rd_vc[n] = v_add
            p.wait("sync", "vc", v_add)
            d_x2 = p.dma("sync", x2_spill[128 * c:128 * (c + 1), :], F(F_X2S, 2048), "dxw")
            if c + 1 < NTC:
                p.wait("sync", "vc", v_add)
                dab[(c + 1) % 2] = p.dma("sync", F(F_XB, 2048),
                                         x[128 * (c + 1):128 * (c + 2), :],
                                         "db0" if (c + 1) % 2 == 0 else "db1")
        if "x2" in taps:
            p.wait("sync", "dxw", p.cnt["dxw"])
            p.dma("sync", tap_out["x2"], x2_spill, "dtap")

        d_x2w = p.cnt["dxw"]
        p.wait("gpsimd", "tn", p.cnt["tn"])
        for k in range(16):
            p.dma("gpsimd", B(C_UA + 512 * k, 512), ua[128 * k:128 * (k + 1), :], "dwa")
        for k in range(4):
            p.dma("gpsimd", B(C_VA + 2048 * k, 2048), va[128 * k:128 * (k + 1), :], "dwa")
        for k in range(16):
            p.dma("gpsimd", B(C_WR + 32 * k, 32), wrt[128 * k:128 * (k + 1), :], "dwa")
        d_cw = p.cnt["dwa"]
        p.wait("sync", "dxw", d_x2w)
        dx2 = [0] * NTC
        d_outs = [0] * NTC
        t_l2tr = [0] * NTC
        t_hus = [0] * NTC

        def xslot(c):
            return F_XB if c % 2 == 0 else F_X2S

        def emit_lnhu(c):
            p.wait("vector", "dc0c" if c % 2 == 0 else "dc1c", dx2[c])
            if c >= 1:
                p.wait("vector", "tn", t_l2tr[c - 1])
            v_ln2 = ln_chain(AF[:, xslot(c):xslot(c) + 2048], B(C_LN2T, 2048), 1e-5)
            p.wait("tensor", "vc", v_ln2)

            def ln2_dst(i0, cnt):
                return B(C_LN2F + 128 * i0, 128 * cnt)
            v_l2f = transpose_tiles(C_LN2T, 16, ln2_dst, bA=5, bB=7)
            t_l2tr[c] = p.cnt["tn"]
            p.wait("tensor", "vc", v_l2f)
            p.guard_banks([4, 6])
            for k in range(16):
                lhs = B(C_LN2F + 128 * k, 128)
                p.op("tensor", "matmul", PB(4), lhs, B(C_UA + 512 * k, 512),
                     start=(k == 0), stop=(k == 15), skip_group_check=True)
                p.op("tensor", "matmul", PB(6, 0, 32), lhs, B(C_WR + 32 * k, 32),
                     start=(k == 0), stop=(k == 15), skip_group_check=True)
            t_hus[c] = p.tpub()

        dx2[0] = p.dma("sync", F(F_XB, 2048), x2_spill[0:128, :], "dc0c")
        p.wait("tensor", "dwa", d_cw)
        emit_lnhu(0)
        for c in range(NTC):
            p.wait("vector", "tn", t_hus[c])
            scp = PB(6, 0, 32)
            p.op("vector", "tensor_reduce", ST(S_M1), scp, AX, ALU.max)
            p.vdrain()
            p.op("vector", "tensor_scalar_mul", ST(S_NEGM1), ST(S_M1), -1.0)
            p.op("vector", "tensor_single_scalar", F(F_MASKF, 32), scp, ST(S_M1),
                 ALU.is_equal)
            p.vdrain()
            p.op("vector", "scalar_tensor_tensor", F(F_SC2, 32), F(F_MASKF, 32),
                 MASKNEG, scp, ALU.mult, ALU.add)
            p.vdrain()
            p.op("vector", "tensor_reduce", ST(S_M2), F(F_SC2, 32), AX, ALU.max)
            v_r1 = p.vpub()
            p.wait("scalar", "vc", v_r1)
            p.op("scalar", "activation", F(F_ESB, 32), scp, ACT.Exp,
                 bias=ST(S_NEGM1), scale=1.0, accum_out=ST(S_ZR))
            s_e = p.spub()
            p.bank_rd_sa[6] = s_e
            p.wait("vector", "sa", s_e)
            p.op("vector", "tensor_single_scalar", F(F_MASKF, 32), scp, ST(S_M2),
                 ALU.is_ge)
            p.vdrain()
            p.op("vector", "scalar_tensor_tensor", F(F_MASKF, 32), F(F_ESB, 32), 1.0,
                 F(F_MASKF, 32), ALU.mult, ALU.mult, accum_out=ST(S_S2G))
            p.vdrain()
            p.op("vector", "tensor_scalar", ST(S_DEN), ST(S_ZR), 1e-8, ST(S_S2G),
                 ALU.mult, ALU.add)
            p.vdrain()
            p.op("vector", "reciprocal", ST(S_REC), ST(S_DEN))
            p.vdrain()
            p.op("vector", "tensor_scalar_mul", B(C_G, 32), F(F_MASKF, 32), ST(S_REC))
            v_g = p.vpub()
            p.bank_rd_vc[6] = v_g
            p.wait("tensor", "vc", v_g)
            p.guard_banks([7])
            p.op("tensor", "matmul", PB(7, 0, 128, 0, NE), B(C_G, 32), B(O_ID, 128),
                 start=True, stop=True, skip_group_check=True)
            t_g = p.tpub()
            p.wait("vector", "tn", t_g)
            p.op("vector", "tensor_copy", B(C_GF, 128, 0, NE), PB(7, 0, 128, 0, NE))
            v_gf = p.vpub()
            p.bank_rd_vc[7] = v_gf
            p.wait("tensor", "vc", v_gf)
            p.guard_banks([5])
            p.op("tensor", "matmul", PB(5), B(C_GF, 128, 0, NE), B(O_EP, 512, 0, NE),
                 start=True, stop=True, skip_group_check=True)
            p.op("tensor", "matmul", PB(7, 128, 1), B(C_GF, 128, 0, NE),
                 B(O_AW, 1, 0, NE), start=True, stop=True, skip_group_check=True)
            t_ge = p.tpub()
            p.wait("vector", "tn", t_ge)
            p.op("vector", "tensor_copy", B(C_GE, 512), PB(5))
            p.vdrain()
            p.op("vector", "tensor_tensor", B(C_ZSB, 512), PB(4), B(C_GE, 512), ALU.mult)
            p.vdrain()
            p.op("vector", "tensor_scalar", ST(S_F), PB(7, 128, 1), 0.2, 0.9,
                 ALU.mult, ALU.add)
            v_z = p.vpub()
            p.bank_rd_vc[4] = v_z
            p.bank_rd_vc[5] = v_z
            p.bank_rd_vc[7] = v_z
            p.wait("tensor", "vc", v_z)
            p.guard_banks([6])
            for kz in range(4):
                p.op("tensor", "matmul", PB(6, 128 * kz, 128), B(C_ZSB + 128 * kz, 128),
                     B(O_ID, 128), start=True, stop=True, skip_group_check=True)
            t_zt = p.tpub()
            p.wait("vector", "tn", t_zt)
            p.op("vector", "tensor_copy", B(C_ZF, 512), PB(6))
            v_zf = p.vpub()
            p.bank_rd_vc[6] = v_zf
            if c + 1 < NTC:
                if c >= 1:
                    p.wait("sync", "dout", d_outs[c - 1])
                dx2[c + 1] = p.dma("sync", AF[:, xslot(c + 1):xslot(c + 1) + 2048],
                                   x2_spill[128 * (c + 1):128 * (c + 2), :],
                                   "dc0c" if (c + 1) % 2 == 0 else "dc1c")
                emit_lnhu(c + 1)
            p.wait("tensor", "vc", v_zf)
            p.guard_banks([0, 1, 2, 3])
            for n in range(4):
                for k in range(4):
                    p.op("tensor", "matmul", PB(n), B(C_ZF + 128 * k, 128),
                         B(C_VA + 2048 * k + 512 * n, 512),
                         start=(k == 0), stop=(k == 3), skip_group_check=True)
            t_f = p.tpub()
            p.wait("vector", "tn", t_f)
            for n in range(4):
                sl = xslot(c) + 512 * n
                p.op("vector", "scalar_tensor_tensor",
                     AF[:, sl:sl + 512], PB(n), ST(S_F),
                     AF[:, sl:sl + 512], ALU.mult, ALU.add)
            v_out = p.vpub()
            for n in range(4):
                p.bank_rd_vc[n] = v_out
            p.wait("sync", "vc", v_out)
            d_outs[c] = p.dma("sync", out[128 * c:128 * (c + 1), :],
                              AF[:, xslot(c):xslot(c) + 2048], "dout")

        for s in p.dma_sems:
            p.wait("sync", s, p.cnt[s])

        with nc.Block() as block:
            @block.sync
            def _(e):
                p.replay("sync", e)

            @block.gpsimd
            def _(e):
                p.replay("gpsimd", e)

            @block.tensor
            def _(e):
                p.replay("tensor", e)

            @block.vector
            def _(e):
                p.replay("vector", e)

            @block.scalar
            def _(e):
                p.replay("scalar", e)

    return nc


def prep_inputs(inputs):
    f32 = np.float32
    qa = np.ascontiguousarray(inputs["L0_attn_q_a_w"], dtype=f32)
    kva = np.ascontiguousarray(inputs["L0_attn_kv_a_w"], dtype=f32)
    qb = np.ascontiguousarray(inputs["L0_attn_q_b_w"], dtype=f32)
    kvb = np.asarray(inputs["L0_attn_kv_b_w"], dtype=f32)
    kvbn = np.ascontiguousarray(kvb.reshape(KL, NH, DN + DV)[:, :, :DN].reshape(KL, NH * DN))
    kvbv = np.ascontiguousarray(kvb.reshape(KL, NH, DN + DV)[:, :, DN:].reshape(KL, NH * DV))
    ow = np.ascontiguousarray(inputs["L0_attn_o_w"], dtype=f32)
    wrt = np.ascontiguousarray(np.asarray(inputs["L0_ffn_Wr"], dtype=f32).T)
    ua = np.ascontiguousarray(np.asarray(inputs["L0_ffn_U"], dtype=f32).transpose(1, 0, 2).reshape(D, NE * RK))
    va = np.ascontiguousarray(np.asarray(inputs["L0_ffn_V"], dtype=f32).reshape(NE * RK, D))
    attrt = np.ascontiguousarray(np.asarray(inputs["L0_ffn_attr"], dtype=f32).T)
    wac = np.ascontiguousarray(np.asarray(inputs["L0_ffn_Wa"], dtype=f32)[0][:, None])
    qln = np.ascontiguousarray(np.asarray(inputs["L0_attn_q_a_ln_w"], dtype=f32).reshape(12, 128).T)
    kvln = np.ascontiguousarray(np.asarray(inputs["L0_attn_kv_a_ln_w"], dtype=f32).reshape(4, 128).T)
    return dict(qa=qa, kva=kva, qb=qb, kvbn=kvbn, kvbv=kvbv, ow=ow, wrt=wrt,
                ua=ua, va=va, attrt=attrt, wac=wac, qln=qln, kvln=kvln)


def kernel(**inputs):
    shared = prep_inputs(inputs)
    x = np.ascontiguousarray(np.asarray(inputs["L0_attn_hidden_states"], dtype=np.float32))
    nc = build()
    in_maps = []
    for c in range(8):
        m = dict(shared)
        m["x"] = np.ascontiguousarray(x[c])
        in_maps.append(m)
    res = run_bass_kernel_spmd(nc, in_maps, core_ids=list(range(8)))
    return np.stack([res.results[c]["out"] for c in range(8)], axis=0)


# version 19
# speedup vs baseline: 1.0051x; 1.0051x over previous
import math
import contextlib
import numpy as np
import ml_dtypes
import concourse.bass as bass
import concourse.mybir as mybir
from concourse.bass_utils import run_bass_kernel_spmd

F32 = mybir.dt.float32
BF16 = mybir.dt.bfloat16
AX = mybir.AxisListType.X
ALU = mybir.AluOpType
ACT = mybir.ActivationFunctionType

T = 2048
D = 2048
NH = 16
DN = 128
DR = 64
DV = 128
QL = 1536
KL = 512
NE = 32
RK = 16
NTC = T // 128
SCALE = float(np.float32(1.0) / np.sqrt(np.float32(DN + DR)))
MASKNEG = -1.0e9

O_ID = 0
O_EP = 128
O_AW = 640
O_ATTRB = 656
O_WAB = 688
O_ONE = 704
O_MB = 720
O_LN1 = 1024
O_QLAT = 33792
O_WBIG = 58368
O_LN1T = 82944
O_QLT = 84992
O_TRB = 86528
O_KVLAT = 88576
NB = 96768
AT_KVBN = 1024
AT_KVBV = 9216
AT_OW = 17408
AT_QN = 50176
AT_QR = 54272
AT_KNH = 58368
AT_KRH = 60416
AT_VH = 64512
AT_PB = 66624
AT_ATS2 = 70720
AT_ABR = 72768
AT_ABF = 76864
C_UA = 1024
C_VA = 9216
C_WR = 17408
C_LN2T = 17920
C_LN2F = 19968
C_ZSB = 22016
C_ZF = 22528
C_GF = 23040
C_G = 23168
C_GE = 23296

F_TRILQ = 0
F_QLN = 128
F_KVLN = 140
F_ST = 144
F_ESB = F_ST + 48
F_MASKF = F_ST + 80
F_SC2 = F_ST + 112
F_XB = 304
F_X2S = 2352
NF = 4400

(S_SSUM, S_M, S_S2, S_VARR, S_RSTD, S_MX, S_NEGMS, S_Z, S_LNZ, S_B2,
 S_M1, S_NEGM1, S_ZR, S_M2, S_S2G, S_DEN, S_REC, S_F, S_SA, S_SB, S_SC,
 S_T0, S_T1) = range(23)

ENGINES = ("sync", "gpsimd", "tensor", "vector", "scalar")


class Prog:
    def __init__(self, nc, es):
        self.nc = nc
        self.sched = {e: [] for e in ENGINES}
        self.semh = {}
        self.cnt = {}
        self.dma_sems = ("dc", "dws", "dwa", "dwo", "dxx", "dkr", "dqs0", "dqs1",
                         "dp0", "dp1", "dat", "db0", "db1", "dxw", "dc0c", "dc1c",
                         "dout", "dtap")
        for s in ("tn", "vc", "sa") + self.dma_sems:
            self.semh[s] = es.enter_context(nc.semaphore(s))
            self.cnt[s] = 0
        self.bank_rd_vc = [0] * 8
        self.bank_rd_sa = [0] * 8

    def op(self, eng, name, *args, inc=None, **kw):
        self.sched[eng].append([name, args, kw, inc])

    def wait(self, eng, sem, val):
        if val <= 0:
            return
        self.sched[eng].append(["__wait__", (sem, val), {}, None])

    def bump(self, sem, n):
        self.cnt[sem] += n
        return self.cnt[sem]

    def dma(self, eng, dst, src, sem):
        v = self.bump(sem, 16)
        self.op(eng, "dma_start", dst, src, inc=(sem, 16))
        return v

    def _pub(self, eng, sem):
        v = self.bump(sem, 1)
        last = self.sched[eng][-1] if self.sched[eng] else None
        if last is not None and last[0] != "__wait__" and last[3] is None:
            last[3] = (sem, 1)
        else:
            self.op(eng, "drain", inc=(sem, 1))
        return v

    def vpub(self):
        return self._pub("vector", "vc")

    def vdrain(self):
        self.op("vector", "drain")

    def spub(self):
        return self._pub("scalar", "sa")

    def tpub(self):
        return self._pub("tensor", "tn")

    def replay(self, eng, e):
        semh = self.semh
        for name, args, kw, inc in self.sched[eng]:
            if name == "__wait__":
                e.wait_ge(semh[args[0]], args[1])
                continue
            ins = getattr(e, name)(*args, **kw)
            if inc is not None:
                ins.then_inc(semh[inc[0]], inc[1])

    def barrier(self, engines=("tensor", "vector", "scalar")):
        for eng in engines:
            for sem in ("tn", "vc", "sa") + self.dma_sems:
                self.wait(eng, sem, self.cnt[sem])

    def guard_banks(self, banks):
        v = max(self.bank_rd_vc[b] for b in banks)
        s = max(self.bank_rd_sa[b] for b in banks)
        self.wait("tensor", "vc", v)
        self.wait("tensor", "sa", s)


def strided_cols(arena, base, n, stride, w, limit=NB):
    if base + n * stride <= limit:
        big = arena[:, base: base + n * stride]
        return big.rearrange("p (f t) -> p f t", t=stride)[:, :, 0:w]
    pad = stride - w
    big = arena[:, base - pad: base - pad + n * stride]
    return big.rearrange("p (f t) -> p f t", t=stride)[:, :, pad:stride]


def build(taps=()):
    nc = bass.Bass("TRN2", target_bir_lowering=False, debug=False)

    x = nc.dram_tensor("x", [T, D], F32, kind="ExternalInput").ap()
    qa = nc.dram_tensor("qa", [D, QL], F32, kind="ExternalInput").ap()
    kva = nc.dram_tensor("kva", [D, QL], F32, kind="ExternalInput").ap()
    qb = nc.dram_tensor("qb", [QL, NH * (DN + DR)], F32, kind="ExternalInput").ap()
    kvbn = nc.dram_tensor("kvbn", [KL, NH * DN], F32, kind="ExternalInput").ap()
    kvbv = nc.dram_tensor("kvbv", [KL, NH * DV], F32, kind="ExternalInput").ap()
    ow = nc.dram_tensor("ow", [NH * DV, D], F32, kind="ExternalInput").ap()
    wrt = nc.dram_tensor("wrt", [D, NE], F32, kind="ExternalInput").ap()
    ua = nc.dram_tensor("ua", [D, NE * RK], F32, kind="ExternalInput").ap()
    va = nc.dram_tensor("va", [NE * RK, D], F32, kind="ExternalInput").ap()
    attrt = nc.dram_tensor("attrt", [NE, NE], F32, kind="ExternalInput").ap()
    wac = nc.dram_tensor("wac", [NE, 1], F32, kind="ExternalInput").ap()
    qln = nc.dram_tensor("qln", [128, 12], F32, kind="ExternalInput").ap()
    kvln = nc.dram_tensor("kvln", [128, 4], F32, kind="ExternalInput").ap()
    out = nc.dram_tensor("out", [T, D], F32, kind="ExternalOutput").ap()

    q_spill = nc.dram_tensor("q_spill", [NH * (DN + DR), T], BF16, kind="Internal").ap()
    kr_spill = nc.dram_tensor("kr_spill", [NH * DR, T], BF16, kind="Internal").ap()
    attn_spill = nc.dram_tensor("attn_spill", [T, NH * DV], BF16, kind="Internal").ap()
    x2_spill = nc.dram_tensor("x2_spill", [T, D], F32, kind="Internal").ap()

    ident_np = np.eye(128, dtype=ml_dtypes.bfloat16)
    ident_h = nc.inline_tensor(ident_np, "identc").ap()
    trilq_np = np.where(np.arange(128)[None, :] >= np.arange(128)[:, None],
                        np.float32(0.0), np.float32(MASKNEG)).astype(np.float32)
    trilq_h = nc.inline_tensor(trilq_np, "trilqc").ap()
    ep_np = np.zeros((NE, NE * RK), dtype=ml_dtypes.bfloat16)
    for s in range(NE):
        ep_np[s, s * RK:(s + 1) * RK] = 1.0
    ep_h = nc.inline_tensor(ep_np, "epc").ap()

    tap_out = {}
    tap_shapes = {"ln1": ([128, 32768], BF16), "qlat": ([128, 24576], BF16),
                  "kvlat": ([128, 8192], BF16), "q_spill": ([NH * (DN + DR), T], BF16),
                  "kr": ([NH * DR, T], BF16), "attn": ([T, NH * DV], BF16),
                  "x2": ([T, D], F32)}
    for t_ in taps:
        sh, dt = tap_shapes[t_]
        tap_out[t_] = nc.dram_tensor("tap_" + t_, sh, dt, kind="ExternalOutput").ap()

    with contextlib.ExitStack() as es:
        AB = es.enter_context(nc.sbuf_tensor("AB", [128, NB], BF16))
        AF = es.enter_context(nc.sbuf_tensor("AF", [128, NF], F32))
        PS = es.enter_context(nc.psum_tensor("PS", [128, 4096], F32))
        p = Prog(nc, es)

        def B(off, w, p0=0, p1=128):
            return AB[p0:p1, off:off + w]

        def F(off, w, p0=0, p1=128):
            return AF[p0:p1, off:off + w]

        def PB(b, o=0, w=512, p0=0, p1=128):
            return PS[p0:p1, 512 * b + o: 512 * b + o + w]

        def ST(i):
            return AF[:, F_ST + i: F_ST + i + 1]

        p.dma("sync", B(O_ID, 128), ident_h, "dc")
        p.dma("sync", F(F_TRILQ, 128), trilq_h, "dc")
        p.dma("sync", B(O_EP, NE * RK, 0, NE), ep_h, "dc")
        p.dma("sync", F(F_QLN, 12), qln, "dc")
        d_consts = p.dma("sync", F(F_KVLN, 4), kvln, "dc")
        p.dma("gpsimd", B(O_ATTRB, NE, 0, NE), attrt, "dws")
        d_wab = p.dma("gpsimd", B(O_WAB, 1, 0, NE), wac, "dws")
        for k in range(16):
            p.dma("gpsimd", B(O_WBIG + 1536 * k, 1536), qa[128 * k:128 * (k + 1), :],
                  "dwa")
        d_qa = p.cnt["dwa"]
        dx0 = p.dma("sync", F(F_XB, 2048), x[0:128, :], "dxx")

        p.wait("tensor", "dws", d_wab)
        p.op("tensor", "matmul", PB(7, 0, 1, 0, NE), B(O_ATTRB, NE, 0, NE),
             B(O_WAB, 1, 0, NE), start=True, stop=True)
        t_aw = p.tpub()
        p.wait("scalar", "tn", t_aw)
        p.op("scalar", "activation", B(O_AW, 1, 0, NE), PB(7, 0, 1, 0, NE), ACT.Sigmoid,
             bias=0.0, scale=1.0)
        p.bank_rd_sa[7] = p.spub()

        def ln_chain(src_f32, dst_bf, eps):
            p.op("vector", "reduce_sum", ST(S_SSUM), src_f32, axis=AX)
            p.vdrain()
            p.op("vector", "tensor_scalar_mul", ST(S_M), ST(S_SSUM), 1.0 / D)
            p.vdrain()
            p.op("vector", "scalar_tensor_tensor", B(O_TRB, 2048), src_f32, ST(S_M),
                 src_f32, ALU.subtract, ALU.mult, accum_out=ST(S_S2))
            p.vdrain()
            p.op("vector", "tensor_scalar", ST(S_VARR), ST(S_S2), 1.0 / D, eps,
                 ALU.mult, ALU.add)
            v = p.vpub()
            p.wait("scalar", "vc", v)
            p.op("scalar", "activation", ST(S_T0), ST(S_VARR), ACT.Ln, bias=0.0, scale=1.0)
            sv = p.spub()
            p.wait("scalar", "sa", sv)
            p.op("scalar", "activation", ST(S_RSTD), ST(S_T0), ACT.Exp, bias=0.0, scale=-0.5)
            s = p.spub()
            p.wait("vector", "sa", s)
            p.op("vector", "tensor_scalar", dst_bf, src_f32, ST(S_M), ST(S_RSTD),
                 ALU.subtract, ALU.mult)
            return p.vpub()

        def transpose_tiles(src_off, ntile, dst_ap_fn, bA=3, bB=7):
            vlast = 0
            for i0 in range(0, ntile, 4):
                cnt = min(4, ntile - i0)
                bk = bA if (i0 // 4) % 2 == 0 else bB
                p.guard_banks([bk])
                for i in range(i0, i0 + cnt):
                    p.op("tensor", "matmul", PB(bk, 128 * (i - i0), 128),
                         B(src_off + 128 * i, 128), B(O_ID, 128), start=True, stop=True,
                         skip_group_check=True)
                tv = p.tpub()
                p.wait("vector", "tn", tv)
                p.op("vector", "tensor_copy", dst_ap_fn(i0, cnt), PB(bk, 0, 128 * cnt))
                vlast = p.vpub()
                p.bank_rd_vc[bk] = vlast
            return vlast

        def rms_chain(src_f32, n, dst_bf, eps):
            p.op("vector", "scalar_tensor_tensor", B(O_TRB, n), src_f32,
                 1.0, src_f32, ALU.mult, ALU.mult, accum_out=ST(S_SB))
            p.vdrain()
            p.op("vector", "tensor_scalar", ST(S_SC), ST(S_SB), 1.0 / n, eps,
                 ALU.mult, ALU.add)
            v = p.vpub()
            p.wait("scalar", "vc", v)
            p.op("scalar", "activation", ST(S_T1), ST(S_SC), ACT.Ln, bias=0.0, scale=1.0)
            sv = p.spub()
            p.wait("scalar", "sa", sv)
            p.op("scalar", "activation", ST(S_LNZ), ST(S_T1), ACT.Exp, bias=0.0, scale=-0.5)
            s = p.spub()
            p.wait("vector", "sa", s)
            p.op("vector", "tensor_scalar_mul", dst_bf, src_f32, ST(S_LNZ))
            return p.vpub()

        ln1_ready = [0] * NTC
        v_ln = [0] * NTC
        t_lntr = [0] * NTC
        dx_x = [0] * (NTC + 1)
        dx_x[0] = dx0
        p.wait("vector", "dc", d_consts)
        p.wait("vector", "dxx", dx_x[0])
        v_ln[0] = ln_chain(F(F_XB, 2048), B(O_LN1T, 2048), 1e-5)

        for c in range(NTC):
            def ln1_dst(i0, cnt, c=c):
                return strided_cols(AB, O_LN1 + 2048 * i0 + 128 * c, cnt, 2048, 128)
            p.wait("tensor", "vc", v_ln[c])
            ln1_ready[c] = transpose_tiles(O_LN1T, 16, ln1_dst)
            t_lntr[c] = p.cnt["tn"]
            p.wait("tensor", "dwa", d_qa)
            p.wait("tensor", "vc", ln1_ready[c])
            p.guard_banks([0, 1, 2])
            for k in range(16):
                lhs = B(O_LN1 + 2048 * k + 128 * c, 128)
                for b in range(3):
                    p.op("tensor", "matmul", PB(b), lhs,
                         B(O_WBIG + 1536 * k + 512 * b, 512),
                         start=(k == 0), stop=(k == 15), skip_group_check=True)
            t_qraw = p.tpub()
            if c + 1 < NTC:
                p.wait("sync", "vc", v_ln[c])
                dx_x[c + 1] = p.dma("sync", F(F_XB, 2048),
                                    x[128 * (c + 1):128 * (c + 2), :], "dxx")
                p.wait("vector", "dxx", dx_x[c + 1])
                p.wait("vector", "tn", t_lntr[c])
                v_ln[c + 1] = ln_chain(F(F_XB, 2048), B(O_LN1T, 2048), 1e-5)
            p.wait("vector", "tn", t_qraw)
            p.op("vector", "tensor_copy", AF[:, F_X2S:F_X2S + QL], PS[:, 0:QL])
            v_ev = p.vpub()
            for b in range(3):
                p.bank_rd_vc[b] = v_ev
            p.vdrain()
            rms_chain(AF[:, F_X2S:F_X2S + QL], QL, B(O_QLT, QL), 1e-6)
            v_qlt = p.cnt["vc"]

            def qlat_dst(i0, cnt, c=c):
                return strided_cols(AB, O_QLAT + 2048 * i0 + 128 * c, cnt, 2048, 128)
            p.wait("tensor", "vc", v_qlt)
            transpose_tiles(O_QLT, 12, qlat_dst)

        if "ln1" in taps:
            p.wait("sync", "vc", p.cnt["vc"])
            p.dma("sync", tap_out["ln1"], AB[:, O_LN1:O_LN1 + 32768], "dtap")
        if "qlat" in taps:
            p.wait("sync", "vc", p.cnt["vc"])
            p.dma("sync", tap_out["qlat"], AB[:, O_QLAT:O_QLAT + 24576], "dtap")

        p.wait("gpsimd", "tn", p.cnt["tn"])
        for k in range(16):
            p.dma("gpsimd", B(O_WBIG + 1536 * k, 1536), kva[128 * k:128 * (k + 1), :],
                  "dwa")
        d_kva = p.cnt["dwa"]

        kr_dma = 0

        def emit_kvmm(c):
            p.wait("tensor", "dwa", d_kva)
            p.guard_banks([0, 1, 2])
            for k in range(16):
                lhs = B(O_LN1 + 2048 * k + 128 * c, 128)
                for b in range(3):
                    p.op("tensor", "matmul", PB(b), lhs,
                         B(O_WBIG + 1536 * k + 512 * b, 512),
                         start=(k == 0), stop=(k == 15), skip_group_check=True)
            return p.tpub()

        t_kv = emit_kvmm(0)
        for c in range(NTC):
            p.wait("vector", "tn", t_kv)
            p.op("vector", "tensor_copy", B(O_TRB, 512), PB(1))
            p.op("vector", "tensor_copy", B(O_TRB + 512, 512), PB(2))
            p.op("vector", "tensor_copy", AF[:, F_X2S:F_X2S + KL], PB(0))
            v_ev = p.vpub()
            for b in range(3):
                p.bank_rd_vc[b] = v_ev
            if c + 1 < NTC:
                t_kv = emit_kvmm(c + 1)
            def kr_dst(i0, cnt):
                return B(O_LN1T + 128 * i0, 128 * cnt)
            p.wait("vector", "dkr", kr_dma)
            p.wait("tensor", "vc", v_ev)
            v_kr = transpose_tiles(O_TRB, 8, kr_dst)
            p.wait("sync", "vc", v_kr)
            src = B(O_LN1T, 1024).rearrange("p (f t) -> p f t", t=128)
            dst = kr_spill[0:1024, 128 * c:128 * (c + 1)].rearrange(
                "(f p) t -> p f t", p=128)
            kr_dma = p.dma("sync", dst, src, "dkr")
            p.vdrain()
            rms_chain(AF[:, F_X2S:F_X2S + KL], KL, B(O_QLT, 512), 1e-6)
            v_klt = p.cnt["vc"]

            def kvlat_dst(i0, cnt, c=c):
                return strided_cols(AB, O_KVLAT + 2048 * i0 + 128 * c, cnt, 2048, 128)
            p.wait("tensor", "vc", v_klt)
            transpose_tiles(O_QLT, 4, kvlat_dst)

        if "kvlat" in taps:
            p.wait("sync", "vc", p.cnt["vc"])
            p.dma("sync", tap_out["kvlat"], AB[:, O_KVLAT:O_KVLAT + 8192], "dtap")
        if "kr" in taps:
            p.wait("sync", "dkr", p.cnt["dkr"])
            p.dma("sync", tap_out["kr"], kr_spill, "dtap")

        p.wait("gpsimd", "tn", p.cnt["tn"])

        def qb_off(k):
            return (O_WBIG + 3072 * k) if k < 8 else (O_LN1 + 3072 * (k - 8))
        for k in range(12):
            p.dma("gpsimd", B(qb_off(k), 3072), qb[128 * k:128 * (k + 1), :], "dwa")
        p.wait("vector", "dwa", p.cnt["dwa"])
        for k in range(12):
            p.op("vector", "tensor_scalar_mul", B(qb_off(k), 3072), B(qb_off(k), 3072),
                 AF[:, F_QLN + k:F_QLN + k + 1])
        v_qb = p.vpub()
        p.wait("tensor", "vc", v_qb)

        stg = [O_LN1T, O_TRB]
        stg_free = [0, 0]
        p.wait("vector", "dkr", p.cnt["dkr"])
        for j in range(24):
            sb = j % 2
            p.wait("vector", "dqs0" if sb == 0 else "dqs1", stg_free[sb])
            for g in range(4):
                bk = g
                p.guard_banks([bk])
                for k in range(12):
                    p.op("tensor", "matmul", PB(bk), B(qb_off(k) + 128 * j, 128),
                         B(O_QLAT + 2048 * k + 512 * g, 512),
                         start=(k == 0), stop=(k == 11), skip_group_check=True)
                tv = p.tpub()
                p.wait("vector", "tn", tv)
                p.op("vector", "tensor_copy", B(stg[sb] + 512 * g, 512), PB(bk))
                p.bank_rd_vc[bk] = p.vpub()
            p.wait("sync", "vc", p.bank_rd_vc[3])
            stg_free[sb] = p.dma("sync", q_spill[128 * j:128 * (j + 1), :],
                                 B(stg[sb], 2048), "dqs0" if sb == 0 else "dqs1")

        if "q_spill" in taps:
            p.wait("sync", "dqs0", p.cnt["dqs0"])
            p.wait("sync", "dqs1", p.cnt["dqs1"])
            p.dma("sync", tap_out["q_spill"], q_spill, "dtap")

        p.wait("gpsimd", "tn", p.cnt["tn"])

        for k in range(4):
            p.dma("gpsimd", B(AT_KVBN + 2048 * k, 2048),
                  kvbn[128 * k:128 * (k + 1), :], "dwa")
        for k in range(4):
            p.dma("gpsimd", B(AT_KVBV + 2048 * k, 2048),
                  kvbv[128 * k:128 * (k + 1), :], "dwa")
        p.wait("vector", "dwa", p.cnt["dwa"])
        for k in range(4):
            p.op("vector", "tensor_scalar_mul", B(AT_KVBN + 2048 * k, 2048),
                 B(AT_KVBN + 2048 * k, 2048), AF[:, F_KVLN + k:F_KVLN + k + 1])
        for k in range(4):
            p.op("vector", "tensor_scalar_mul", B(AT_KVBV + 2048 * k, 2048),
                 B(AT_KVBV + 2048 * k, 2048), AF[:, F_KVLN + k:F_KVLN + k + 1])
        p.op("vector", "memset", B(O_ONE, 1), 1.0)
        p.op("vector", "memset", B(O_MB + 8, 128, 0, 1), 1.0)
        p.op("vector", "memset", strided_cols(AB, AT_VH + 128, 16, 132, 1), 1.0)
        v_kvb = p.vpub()
        d_ow = 0
        for k in range(16):
            d_ow = p.dma("gpsimd", B(AT_OW + 2048 * k, 2048),
                         ow[128 * k:128 * (k + 1), :], "dwo")

        dq = [0, 0]
        p.wait("sync", "dqs0", p.cnt["dqs0"])
        p.wait("sync", "dqs1", p.cnt["dqs1"])
        p.wait("sync", "dkr", p.cnt["dkr"])
        p.dma("sync", B(AT_QN, 2048), q_spill[0:128, :], "dp0")
        p.dma("sync", B(AT_QR, 2048, 0, 64), q_spill[128:192, :], "dp0")
        dq[0] = p.dma("sync", B(AT_KRH, 2048, 0, 64), kr_spill[0:64, :], "dp0")

        p.wait("tensor", "vc", v_kvb)
        att_dma = 0
        ebuf_free = [0, 0]
        for h in range(NH):
            bufq = h % 2
            qn = AT_QN + 2048 * bufq
            qr = AT_QR + 2048 * bufq
            krh = AT_KRH + 2048 * bufq
            p.wait("tensor", "dp0" if bufq == 0 else "dp1", dq[bufq])
            p.guard_banks(list(range(8)))
            p.guard_banks([0, 1, 2, 3])
            for n in range(4):
                for kk in range(4):
                    p.op("tensor", "matmul", PB(n), B(AT_KVBN + 2048 * kk + 128 * h, 128),
                         B(O_KVLAT + 2048 * kk + 512 * n, 512),
                         start=(kk == 0), stop=(kk == 3), skip_group_check=True)
            t_kn = p.tpub()
            p.wait("vector", "tn", t_kn)
            for n in range(4):
                p.op("vector", "tensor_copy", B(AT_KNH + 512 * n, 512), PB(n))
            v_kn = p.vpub()
            for n in range(4):
                p.bank_rd_vc[n] = v_kn
            if h + 1 < NH:
                hn = h + 1
                nb = hn % 2
                p.wait("sync", "tn", p.cnt["tn"])
                psem = "dp0" if nb == 0 else "dp1"
                p.dma("sync", B(AT_QN + 2048 * nb, 2048),
                      q_spill[192 * hn:192 * hn + 128, :], psem)
                p.dma("sync", B(AT_QR + 2048 * nb, 2048, 0, 64),
                      q_spill[192 * hn + 128:192 * hn + 192, :], psem)
                dq[nb] = p.dma("sync", B(AT_KRH + 2048 * nb, 2048, 0, 64),
                               kr_spill[64 * hn:64 * hn + 64, :], psem)
            sqb = AT_PB
            p.wait("vector", "dp0" if bufq == 0 else "dp1", dq[bufq])
            p.wait("vector", "tn", max(ebuf_free))
            p.op("vector", "tensor_tensor", B(sqb, 2048), B(qn, 2048), B(qn, 2048),
                 ALU.mult)
            v1 = p.vpub()
            p.wait("tensor", "vc", max(v1, v_kn))
            p.guard_banks([0, 1, 2, 3])
            for n in range(4):
                p.op("tensor", "matmul", PS[0:1, 512 * n:512 * n + 512],
                     B(O_ONE, 1), B(sqb + 512 * n, 512),
                     start=True, stop=False, skip_group_check=True)
            t_qa = p.tpub()
            p.wait("vector", "tn", t_qa)
            p.op("vector", "tensor_tensor", B(sqb, 2048, 0, 64), B(qr, 2048, 0, 64),
                 B(qr, 2048, 0, 64), ALU.mult)
            v2 = p.vpub()
            p.wait("tensor", "vc", v2)
            for n in range(4):
                p.op("tensor", "matmul", PS[0:1, 512 * n:512 * n + 512],
                     B(O_ONE, 1, 0, 64), B(sqb + 512 * n, 512, 0, 64),
                     start=False, stop=True, skip_group_check=True)
            t_q2 = p.tpub()
            p.wait("vector", "tn", t_q2)
            p.op("vector", "tensor_reduce", AF[0:1, F_ST + S_MX:F_ST + S_MX + 1],
                 PS[0:1, 0:2048], AX, ALU.max)
            v_qm = p.vpub()
            for n in range(4):
                p.bank_rd_vc[n] = v_qm
            p.vdrain()
            p.op("vector", "tensor_tensor", B(sqb, 2048), B(AT_KNH, 2048),
                 B(AT_KNH, 2048), ALU.mult)
            v3 = p.vpub()
            p.wait("tensor", "vc", v3)
            p.guard_banks([0, 1, 2, 3])
            for n in range(4):
                p.op("tensor", "matmul", PS[0:1, 512 * n:512 * n + 512],
                     B(O_ONE, 1), B(sqb + 512 * n, 512),
                     start=True, stop=False, skip_group_check=True)
            t_ka = p.tpub()
            p.wait("vector", "tn", t_ka)
            p.op("vector", "tensor_tensor", B(sqb, 2048, 0, 64), B(krh, 2048, 0, 64),
                 B(krh, 2048, 0, 64), ALU.mult)
            v4 = p.vpub()
            p.wait("tensor", "vc", v4)
            for n in range(4):
                p.op("tensor", "matmul", PS[0:1, 512 * n:512 * n + 512],
                     B(O_ONE, 1, 0, 64), B(sqb + 512 * n, 512, 0, 64),
                     start=False, stop=True, skip_group_check=True)
            t_k2 = p.tpub()
            ebuf_free[0] = max(ebuf_free[0], t_k2)
            v_vh = 0
            for tg in range(4):
                bk = 4 + (tg % 2)
                p.guard_banks([bk])
                for tc in range(4):
                    tt = 4 * tg + tc
                    for kk in range(4):
                        p.op("tensor", "matmul", PB(bk, 128 * tc, 128),
                             B(O_KVLAT + 2048 * kk + 128 * tt, 128),
                             B(AT_KVBV + 2048 * kk + 128 * h, 128),
                             start=(tc == 0 and kk == 0), stop=(tc == 3 and kk == 3),
                             skip_group_check=True)
                tv = p.tpub()
                p.wait("vector", "tn", tv)
                p.op("vector", "tensor_copy",
                     strided_cols(AB, AT_VH + 132 * 4 * tg, 4, 132, 128),
                     PB(bk, 0, 512))
                v_vh = p.vpub()
                p.bank_rd_vc[bk] = v_vh
            p.wait("vector", "tn", t_k2)
            p.op("vector", "tensor_reduce", AF[0:1, F_ST + S_Z:F_ST + S_Z + 1],
                 PS[0:1, 0:2048], AX, ALU.max)
            v_km = p.vpub()
            for n in range(4):
                p.bank_rd_vc[n] = v_km
            p.wait("scalar", "vc", v_km)
            p.op("scalar", "activation", AF[0:1, F_ST + S_T0:F_ST + S_T0 + 1],
                 AF[0:1, F_ST + S_MX:F_ST + S_MX + 1], ACT.Ln, bias=0.0, scale=1.0)
            p.op("scalar", "activation", AF[0:1, F_ST + S_T1:F_ST + S_T1 + 1],
                 AF[0:1, F_ST + S_Z:F_ST + S_Z + 1], ACT.Ln, bias=0.0, scale=1.0)
            s_ln = p.spub()
            p.wait("vector", "sa", s_ln)
            p.op("vector", "tensor_add", AF[0:1, F_ST + S_B2:F_ST + S_B2 + 1],
                 AF[0:1, F_ST + S_T0:F_ST + S_T0 + 1],
                 AF[0:1, F_ST + S_T1:F_ST + S_T1 + 1])
            v_b = p.vpub()
            p.wait("scalar", "vc", v_b)
            p.op("scalar", "activation", AF[0:1, F_ST + S_REC:F_ST + S_REC + 1],
                 AF[0:1, F_ST + S_B2:F_ST + S_B2 + 1], ACT.Exp, bias=0.0, scale=0.5)
            s_mh = p.spub()
            p.wait("vector", "sa", s_mh)
            p.op("vector", "tensor_scalar_mul", B(O_MB, 1, 0, 1),
                 AF[0:1, F_ST + S_REC:F_ST + S_REC + 1], -SCALE)
            v_mb = p.vpub()
            p.wait("tensor", "vc", v_mb)
            p.guard_banks([0])
            p.op("tensor", "matmul", PB(0, 0, 1), B(O_MB + 8, 128, 0, 1),
                 B(O_MB, 1, 0, 1), start=True, stop=True, skip_group_check=True)
            t_bc = p.tpub()
            p.wait("vector", "tn", t_bc)
            p.op("vector", "tensor_copy", ST(S_NEGMS), PB(0, 0, 1))
            v_bias = p.vpub()
            p.bank_rd_vc[0] = v_bias
            p.wait("scalar", "vc", v_bias)
            def emit_pv(j, eb, qg0, s_exp):
                for i in range(j, NTC):
                    if i == j or (i % 4 == 0 and i // 4 > qg0):
                        p.wait("tensor", "sa", s_exp[i // 4])
                    bkk = 2 + i // 3
                    off = 129 * (i % 3)
                    p.op("tensor", "matmul",
                         PS[:, 512 * bkk + off: 512 * bkk + off + 129],
                         B(eb + 128 * i, 128), B(AT_VH + 132 * j, 129),
                         start=(j == 0 and i % 3 == 0), stop=False,
                         skip_group_check=True)
                ebuf_free[j % 2] = p.tpub()

            p.wait("tensor", "vc", v_vh)
            pv_args = None
            for j in range(NTC):
                eb = AT_PB + 2048 * (j % 2)
                qg0 = (128 * j) // 512
                s_exp = [0] * 4
                for qg in range(qg0, 4):
                    bk = qg % 2
                    p.guard_banks([bk])
                    p.op("tensor", "matmul", PB(bk), B(AT_KNH + 128 * j, 128),
                         B(qn + 512 * qg, 512), start=True, stop=False,
                         skip_group_check=True)
                    p.op("tensor", "matmul", PB(bk), B(krh + 128 * j, 128, 0, 64),
                         B(qr + 512 * qg, 512, 0, 64), start=False, stop=True,
                         skip_group_check=True)
                    t_st = p.tpub()
                    if qg == qg0:
                        p.wait("vector", "tn", t_st)
                        dg = PB(bk, 128 * (j % 4), 128)
                        p.op("vector", "tensor_add", dg, dg, F(F_TRILQ, 128))
                        v_mask = p.vpub()
                        p.bank_rd_vc[bk] = v_mask
                        p.wait("scalar", "vc", v_mask)
                    p.wait("scalar", "tn", max(t_st, ebuf_free[j % 2]))
                    p.op("scalar", "activation", B(eb + 512 * qg, 512), PB(bk),
                         ACT.Exp, bias=ST(S_NEGMS), scale=SCALE)
                    s_exp[qg] = p.spub()
                    p.bank_rd_sa[bk] = s_exp[qg]
                if pv_args is not None:
                    emit_pv(*pv_args)
                pv_args = (j, eb, qg0, list(s_exp))
            emit_pv(*pv_args)
            t_pv = ebuf_free[(NTC - 1) % 2]
            p.wait("vector", "tn", t_pv)
            p.wait("vector", "dat", att_dma)
            for b in range(6):
                cnt = 3 if b < 5 else 1
                srcz = PS[:, 512 * (2 + b): 512 * (2 + b) + 129 * cnt].rearrange(
                    "p (t s) -> p t s", s=129)[:, :, 128:129]
                dstz = AF[:, F_ESB + 3 * b: F_ESB + 3 * b + cnt].rearrange(
                    "p (t s) -> p t s", s=1)
                p.op("vector", "tensor_copy", dstz, srcz)
            v_zg = p.vpub()
            p.wait("scalar", "vc", v_zg)
            p.op("scalar", "activation", F(F_MASKF, 16), F(F_ESB, 16), ACT.Ln,
                 bias=0.0, scale=1.0)
            sv = p.spub()
            p.wait("scalar", "sa", sv)
            p.op("scalar", "activation", F(F_SC2, 16), F(F_MASKF, 16), ACT.Exp,
                 bias=0.0, scale=-1.0)
            s_rz = p.spub()
            p.wait("vector", "sa", s_rz)
            for i in range(NTC):
                bkk = 2 + i // 3
                off = 129 * (i % 3)
                p.op("vector", "tensor_scalar_mul", B(AT_ATS2 + 128 * i, 128),
                     PS[:, 512 * bkk + off: 512 * bkk + off + 128],
                     AF[:, F_SC2 + i: F_SC2 + i + 1])
            v_at = p.vpub()
            for bq in range(2, 8):
                p.bank_rd_vc[bq] = v_at
            p.wait("sync", "vc", v_at)
            dst = attn_spill[0:T, 128 * h:128 * (h + 1)].rearrange(
                "(i p) v -> p i v", p=128)
            src2 = B(AT_ATS2, 2048).rearrange("p (i v) -> p i v", v=128)
            att_dma = p.dma("sync", dst, src2, "dat")

        if "attn" in taps:
            p.wait("sync", "dat", p.cnt["dat"])
            p.dma("sync", tap_out["attn"], attn_spill, "dtap")

        p.wait("tensor", "dwo", d_ow)
        p.guard_banks(list(range(8)))
        dab = [0, 0]
        p.wait("sync", "dat", att_dma)
        p.dma("sync", B(AT_ABR, 2048), attn_spill[0:128, :], "db0")
        dab[0] = p.dma("sync", F(F_XB, 2048), x[0:128, :], "db0")
        v_add = 0
        d_x2 = 0
        for c in range(NTC):
            sb = c % 2
            p.wait("tensor", "db0" if sb == 0 else "db1", dab[sb])

            def abf_dst(i0, cnt, sb=sb):
                return B(AT_ABF + 2048 * sb + 128 * i0, 128 * cnt)
            v_abf = transpose_tiles(AT_ABR + 2048 * sb, 16, abf_dst, bA=5, bB=7)
            p.wait("tensor", "vc", v_abf)
            p.guard_banks([0, 1, 2, 3])
            for n in range(4):
                for k in range(16):
                    p.op("tensor", "matmul", PB(n),
                         B(AT_ABF + 2048 * sb + 128 * k, 128),
                         B(AT_OW + 2048 * k + 512 * n, 512),
                         start=(k == 0), stop=(k == 15), skip_group_check=True)
            t_o = p.tpub()
            if c + 1 < NTC:
                nb = (c + 1) % 2
                p.wait("sync", "tn", t_o)
                p.dma("sync", B(AT_ABR + 2048 * nb, 2048),
                      attn_spill[128 * (c + 1):128 * (c + 2), :],
                      "db0" if nb == 0 else "db1")
            p.wait("vector", "tn", t_o)
            p.wait("vector", "dxw", d_x2)
            for n in range(4):
                p.op("vector", "tensor_add", AF[:, F_X2S + 512 * n:F_X2S + 512 * n + 512],
                     PB(n), AF[:, F_XB + 512 * n:F_XB + 512 * n + 512])
            v_add = p.vpub()
            for n in range(4):
                p.bank_rd_vc[n] = v_add
            p.wait("sync", "vc", v_add)
            d_x2 = p.dma("sync", x2_spill[128 * c:128 * (c + 1), :], F(F_X2S, 2048), "dxw")
            if c + 1 < NTC:
                p.wait("sync", "vc", v_add)
                dab[(c + 1) % 2] = p.dma("sync", F(F_XB, 2048),
                                         x[128 * (c + 1):128 * (c + 2), :],
                                         "db0" if (c + 1) % 2 == 0 else "db1")
        if "x2" in taps:
            p.wait("sync", "dxw", p.cnt["dxw"])
            p.dma("sync", tap_out["x2"], x2_spill, "dtap")

        d_x2w = p.cnt["dxw"]
        p.wait("gpsimd", "tn", p.cnt["tn"])
        for k in range(16):
            p.dma("gpsimd", B(C_UA + 512 * k, 512), ua[128 * k:128 * (k + 1), :], "dwa")
        for k in range(4):
            p.dma("gpsimd", B(C_VA + 2048 * k, 2048), va[128 * k:128 * (k + 1), :], "dwa")
        for k in range(16):
            p.dma("gpsimd", B(C_WR + 32 * k, 32), wrt[128 * k:128 * (k + 1), :], "dwa")
        d_cw = p.cnt["dwa"]
        p.wait("sync", "dxw", d_x2w)
        dx2 = [0] * NTC
        d_outs = [0] * NTC
        t_l2tr = [0] * NTC
        t_hus = [0] * NTC

        def xslot(c):
            return F_XB if c % 2 == 0 else F_X2S

        def emit_lnhu(c):
            p.wait("vector", "dc0c" if c % 2 == 0 else "dc1c", dx2[c])
            if c >= 1:
                p.wait("vector", "tn", t_l2tr[c - 1])
            v_ln2 = ln_chain(AF[:, xslot(c):xslot(c) + 2048], B(C_LN2T, 2048), 1e-5)
            p.wait("tensor", "vc", v_ln2)

            def ln2_dst(i0, cnt):
                return B(C_LN2F + 128 * i0, 128 * cnt)
            v_l2f = transpose_tiles(C_LN2T, 16, ln2_dst, bA=5, bB=7)
            t_l2tr[c] = p.cnt["tn"]
            p.wait("tensor", "vc", v_l2f)
            p.guard_banks([4, 6])
            for k in range(16):
                lhs = B(C_LN2F + 128 * k, 128)
                p.op("tensor", "matmul", PB(4), lhs, B(C_UA + 512 * k, 512),
                     start=(k == 0), stop=(k == 15), skip_group_check=True)
                p.op("tensor", "matmul", PB(6, 0, 32), lhs, B(C_WR + 32 * k, 32),
                     start=(k == 0), stop=(k == 15), skip_group_check=True)
            t_hus[c] = p.tpub()

        dx2[0] = p.dma("sync", F(F_XB, 2048), x2_spill[0:128, :], "dc0c")
        p.wait("tensor", "dwa", d_cw)
        emit_lnhu(0)
        for c in range(NTC):
            p.wait("vector", "tn", t_hus[c])
            scp = PB(6, 0, 32)
            p.op("vector", "tensor_reduce", ST(S_M1), scp, AX, ALU.max)
            p.vdrain()
            p.op("vector", "tensor_scalar_mul", ST(S_NEGM1), ST(S_M1), -1.0)
            p.op("vector", "tensor_single_scalar", F(F_MASKF, 32), scp, ST(S_M1),
                 ALU.is_equal)
            p.vdrain()
            p.op("vector", "scalar_tensor_tensor", F(F_SC2, 32), F(F_MASKF, 32),
                 MASKNEG, scp, ALU.mult, ALU.add)
            p.vdrain()
            p.op("vector", "tensor_reduce", ST(S_M2), F(F_SC2, 32), AX, ALU.max)
            v_r1 = p.vpub()
            p.wait("scalar", "vc", v_r1)
            p.op("scalar", "activation", F(F_ESB, 32), scp, ACT.Exp,
                 bias=ST(S_NEGM1), scale=1.0, accum_out=ST(S_ZR))
            s_e = p.spub()
            p.bank_rd_sa[6] = s_e
            p.wait("vector", "sa", s_e)
            p.op("vector", "tensor_single_scalar", F(F_MASKF, 32), scp, ST(S_M2),
                 ALU.is_ge)
            p.vdrain()
            p.op("vector", "scalar_tensor_tensor", F(F_MASKF, 32), F(F_ESB, 32), 1.0,
                 F(F_MASKF, 32), ALU.mult, ALU.mult, accum_out=ST(S_S2G))
            p.vdrain()
            p.op("vector", "tensor_scalar", ST(S_DEN), ST(S_ZR), 1e-8, ST(S_S2G),
                 ALU.mult, ALU.add)
            v_den = p.vpub()
            p.wait("scalar", "vc", v_den)
            p.op("scalar", "activation", ST(S_T0), ST(S_DEN), ACT.Ln, bias=0.0,
                 scale=1.0)
            sv2 = p.spub()
            p.wait("scalar", "sa", sv2)
            p.op("scalar", "activation", ST(S_REC), ST(S_T0), ACT.Exp, bias=0.0,
                 scale=-1.0)
            s_rec = p.spub()
            p.wait("vector", "sa", s_rec)
            p.op("vector", "tensor_scalar_mul", B(C_G, 32), F(F_MASKF, 32), ST(S_REC))
            v_g = p.vpub()
            p.bank_rd_vc[6] = v_g
            p.wait("tensor", "vc", v_g)
            p.guard_banks([7])
            p.op("tensor", "matmul", PB(7, 0, 128, 0, NE), B(C_G, 32), B(O_ID, 128),
                 start=True, stop=True, skip_group_check=True)
            t_g = p.tpub()
            p.wait("vector", "tn", t_g)
            p.op("vector", "tensor_copy", B(C_GF, 128, 0, NE), PB(7, 0, 128, 0, NE))
            v_gf = p.vpub()
            p.bank_rd_vc[7] = v_gf
            p.wait("tensor", "vc", v_gf)
            p.guard_banks([5])
            p.op("tensor", "matmul", PB(5), B(C_GF, 128, 0, NE), B(O_EP, 512, 0, NE),
                 start=True, stop=True, skip_group_check=True)
            p.op("tensor", "matmul", PB(7, 128, 1), B(C_GF, 128, 0, NE),
                 B(O_AW, 1, 0, NE), start=True, stop=True, skip_group_check=True)
            t_ge = p.tpub()
            p.wait("vector", "tn", t_ge)
            p.op("vector", "tensor_copy", B(C_GE, 512), PB(5))
            p.vdrain()
            p.op("vector", "tensor_tensor", B(C_ZSB, 512), PB(4), B(C_GE, 512), ALU.mult)
            p.vdrain()
            p.op("vector", "tensor_scalar", ST(S_F), PB(7, 128, 1), 0.2, 0.9,
                 ALU.mult, ALU.add)
            v_z = p.vpub()
            p.bank_rd_vc[4] = v_z
            p.bank_rd_vc[5] = v_z
            p.bank_rd_vc[7] = v_z
            p.wait("tensor", "vc", v_z)
            p.guard_banks([6])
            for kz in range(4):
                p.op("tensor", "matmul", PB(6, 128 * kz, 128), B(C_ZSB + 128 * kz, 128),
                     B(O_ID, 128), start=True, stop=True, skip_group_check=True)
            t_zt = p.tpub()
            p.wait("vector", "tn", t_zt)
            p.op("vector", "tensor_copy", B(C_ZF, 512), PB(6))
            v_zf = p.vpub()
            p.bank_rd_vc[6] = v_zf
            if c + 1 < NTC:
                if c >= 1:
                    p.wait("sync", "dout", d_outs[c - 1])
                dx2[c + 1] = p.dma("sync", AF[:, xslot(c + 1):xslot(c + 1) + 2048],
                                   x2_spill[128 * (c + 1):128 * (c + 2), :],
                                   "dc0c" if (c + 1) % 2 == 0 else "dc1c")
                emit_lnhu(c + 1)
            p.wait("tensor", "vc", v_zf)
            p.guard_banks([0, 1, 2, 3])
            for n in range(4):
                for k in range(4):
                    p.op("tensor", "matmul", PB(n), B(C_ZF + 128 * k, 128),
                         B(C_VA + 2048 * k + 512 * n, 512),
                         start=(k == 0), stop=(k == 3), skip_group_check=True)
            t_f = p.tpub()
            p.wait("vector", "tn", t_f)
            for n in range(4):
                sl = xslot(c) + 512 * n
                p.op("vector", "scalar_tensor_tensor",
                     AF[:, sl:sl + 512], PB(n), ST(S_F),
                     AF[:, sl:sl + 512], ALU.mult, ALU.add)
            v_out = p.vpub()
            for n in range(4):
                p.bank_rd_vc[n] = v_out
            p.wait("sync", "vc", v_out)
            d_outs[c] = p.dma("sync", out[128 * c:128 * (c + 1), :],
                              AF[:, xslot(c):xslot(c) + 2048], "dout")

        for s in p.dma_sems:
            p.wait("sync", s, p.cnt[s])

        with nc.Block() as block:
            @block.sync
            def _(e):
                p.replay("sync", e)

            @block.gpsimd
            def _(e):
                p.replay("gpsimd", e)

            @block.tensor
            def _(e):
                p.replay("tensor", e)

            @block.vector
            def _(e):
                p.replay("vector", e)

            @block.scalar
            def _(e):
                p.replay("scalar", e)

    return nc


def prep_inputs(inputs):
    f32 = np.float32
    qa = np.ascontiguousarray(inputs["L0_attn_q_a_w"], dtype=f32)
    kva = np.ascontiguousarray(inputs["L0_attn_kv_a_w"], dtype=f32)
    qb = np.ascontiguousarray(inputs["L0_attn_q_b_w"], dtype=f32)
    kvb = np.asarray(inputs["L0_attn_kv_b_w"], dtype=f32)
    kvbn = np.ascontiguousarray(kvb.reshape(KL, NH, DN + DV)[:, :, :DN].reshape(KL, NH * DN))
    kvbv = np.ascontiguousarray(kvb.reshape(KL, NH, DN + DV)[:, :, DN:].reshape(KL, NH * DV))
    ow = np.ascontiguousarray(inputs["L0_attn_o_w"], dtype=f32)
    wrt = np.ascontiguousarray(np.asarray(inputs["L0_ffn_Wr"], dtype=f32).T)
    ua = np.ascontiguousarray(np.asarray(inputs["L0_ffn_U"], dtype=f32).transpose(1, 0, 2).reshape(D, NE * RK))
    va = np.ascontiguousarray(np.asarray(inputs["L0_ffn_V"], dtype=f32).reshape(NE * RK, D))
    attrt = np.ascontiguousarray(np.asarray(inputs["L0_ffn_attr"], dtype=f32).T)
    wac = np.ascontiguousarray(np.asarray(inputs["L0_ffn_Wa"], dtype=f32)[0][:, None])
    qln = np.ascontiguousarray(np.asarray(inputs["L0_attn_q_a_ln_w"], dtype=f32).reshape(12, 128).T)
    kvln = np.ascontiguousarray(np.asarray(inputs["L0_attn_kv_a_ln_w"], dtype=f32).reshape(4, 128).T)
    return dict(qa=qa, kva=kva, qb=qb, kvbn=kvbn, kvbv=kvbv, ow=ow, wrt=wrt,
                ua=ua, va=va, attrt=attrt, wac=wac, qln=qln, kvln=kvln)


def kernel(**inputs):
    shared = prep_inputs(inputs)
    x = np.ascontiguousarray(np.asarray(inputs["L0_attn_hidden_states"], dtype=np.float32))
    nc = build()
    in_maps = []
    for c in range(8):
        m = dict(shared)
        m["x"] = np.ascontiguousarray(x[c])
        in_maps.append(m)
    res = run_bass_kernel_spmd(nc, in_maps, core_ids=list(range(8)))
    return np.stack([res.results[c]["out"] for c in range(8)], axis=0)


# version 20
# speedup vs baseline: 1.0866x; 1.0811x over previous
import math
import contextlib
import numpy as np
import ml_dtypes
import concourse.bass as bass
import concourse.mybir as mybir
from concourse.bass_utils import run_bass_kernel_spmd

F32 = mybir.dt.float32
BF16 = mybir.dt.bfloat16
AX = mybir.AxisListType.X
ALU = mybir.AluOpType
ACT = mybir.ActivationFunctionType

T = 2048
D = 2048
NH = 16
DN = 128
DR = 64
DV = 128
QL = 1536
KL = 512
NE = 32
RK = 16
NTC = T // 128
SCALE = float(np.float32(1.0) / np.sqrt(np.float32(DN + DR)))
MASKNEG = -1.0e9

O_ID = 0
O_EP = 128
O_AW = 640
O_ATTRB = 656
O_WAB = 688
O_ONE = 704
O_MB = 720
O_LN1 = 1024
O_QLAT = 33792
O_WBIG = 58368
O_LN1T = 82944
O_QLT = 84992
O_TRB = 86528
O_KVLAT = 88576
NB = 96768
AT_KVBN = 1024
AT_KVBV = 9216
AT_OW = 17408
AT_QN = 50176
AT_QR = 54272
AT_KNH = 58368
AT_KRH = 60416
AT_VH = 64512
AT_PB = 66624
AT_ATS2 = 70720
AT_ABR = 72768
AT_ABF = 76864
C_UA = 1024
C_VA = 9216
C_WR = 17408
C_LN2T = 17920
C_LN2F = 19968
C_ZSB = 22016
C_ZF = 22528
C_GF = 23040
C_G = 23168
C_GE = 23296

F_TRILQ = 0
F_QLN = 128
F_KVLN = 140
F_ST = 144
F_ESB = F_ST + 48
F_MASKF = F_ST + 80
F_SC2 = F_ST + 112
F_XB = 304
F_X2S = 2352
NF = 4400

(S_SSUM, S_M, S_S2, S_VARR, S_RSTD, S_MX, S_NEGMS, S_Z, S_LNZ, S_B2,
 S_M1, S_NEGM1, S_ZR, S_M2, S_S2G, S_DEN, S_REC, S_F, S_SA, S_SB, S_SC,
 S_T0, S_T1) = range(23)

ENGINES = ("sync", "gpsimd", "tensor", "vector", "scalar")


class Prog:
    def __init__(self, nc, es):
        self.nc = nc
        self.sched = {e: [] for e in ENGINES}
        self.semh = {}
        self.cnt = {}
        self.dma_sems = ("dc", "dws", "dwa", "dwo", "dxx", "dkr", "dqs0", "dqs1",
                         "dp0", "dp1", "dat", "db0", "db1", "dxw", "dc0c", "dc1c",
                         "dout", "dtap")
        for s in ("tn", "vc", "sa") + self.dma_sems:
            self.semh[s] = es.enter_context(nc.semaphore(s))
            self.cnt[s] = 0
        self.bank_rd_vc = [0] * 8
        self.bank_rd_sa = [0] * 8

    def op(self, eng, name, *args, inc=None, **kw):
        self.sched[eng].append([name, args, kw, inc])

    def wait(self, eng, sem, val):
        if val <= 0:
            return
        self.sched[eng].append(["__wait__", (sem, val), {}, None])

    def bump(self, sem, n):
        self.cnt[sem] += n
        return self.cnt[sem]

    def dma(self, eng, dst, src, sem):
        v = self.bump(sem, 16)
        self.op(eng, "dma_start", dst, src, inc=(sem, 16))
        return v

    def _pub(self, eng, sem):
        v = self.bump(sem, 1)
        last = self.sched[eng][-1] if self.sched[eng] else None
        if last is not None and last[0] != "__wait__" and last[3] is None:
            last[3] = (sem, 1)
        else:
            self.op(eng, "drain", inc=(sem, 1))
        return v

    def vpub(self):
        return self._pub("vector", "vc")

    def vdrain(self):
        self.op("vector", "drain")

    def spub(self):
        return self._pub("scalar", "sa")

    def tpub(self):
        return self._pub("tensor", "tn")

    def replay(self, eng, e):
        semh = self.semh
        for name, args, kw, inc in self.sched[eng]:
            if name == "__wait__":
                e.wait_ge(semh[args[0]], args[1])
                continue
            ins = getattr(e, name)(*args, **kw)
            if inc is not None:
                ins.then_inc(semh[inc[0]], inc[1])

    def barrier(self, engines=("tensor", "vector", "scalar")):
        for eng in engines:
            for sem in ("tn", "vc", "sa") + self.dma_sems:
                self.wait(eng, sem, self.cnt[sem])

    def guard_banks(self, banks):
        v = max(self.bank_rd_vc[b] for b in banks)
        s = max(self.bank_rd_sa[b] for b in banks)
        self.wait("tensor", "vc", v)
        self.wait("tensor", "sa", s)


def strided_cols(arena, base, n, stride, w, limit=NB):
    if base + n * stride <= limit:
        big = arena[:, base: base + n * stride]
        return big.rearrange("p (f t) -> p f t", t=stride)[:, :, 0:w]
    pad = stride - w
    big = arena[:, base - pad: base - pad + n * stride]
    return big.rearrange("p (f t) -> p f t", t=stride)[:, :, pad:stride]


def build(taps=()):
    nc = bass.Bass("TRN2", target_bir_lowering=False, debug=False)

    x = nc.dram_tensor("x", [T, D], F32, kind="ExternalInput").ap()
    qa = nc.dram_tensor("qa", [D, QL], F32, kind="ExternalInput").ap()
    kva = nc.dram_tensor("kva", [D, QL], F32, kind="ExternalInput").ap()
    qb = nc.dram_tensor("qb", [QL, NH * (DN + DR)], F32, kind="ExternalInput").ap()
    kvbn = nc.dram_tensor("kvbn", [KL, NH * DN], F32, kind="ExternalInput").ap()
    kvbv = nc.dram_tensor("kvbv", [KL, NH * DV], F32, kind="ExternalInput").ap()
    ow = nc.dram_tensor("ow", [NH * DV, D], F32, kind="ExternalInput").ap()
    wrt = nc.dram_tensor("wrt", [D, NE], F32, kind="ExternalInput").ap()
    ua = nc.dram_tensor("ua", [D, NE * RK], F32, kind="ExternalInput").ap()
    va = nc.dram_tensor("va", [NE * RK, D], F32, kind="ExternalInput").ap()
    attrt = nc.dram_tensor("attrt", [NE, NE], F32, kind="ExternalInput").ap()
    wac = nc.dram_tensor("wac", [NE, 1], F32, kind="ExternalInput").ap()
    qln = nc.dram_tensor("qln", [128, 12], F32, kind="ExternalInput").ap()
    kvln = nc.dram_tensor("kvln", [128, 4], F32, kind="ExternalInput").ap()
    out = nc.dram_tensor("out", [T, D], F32, kind="ExternalOutput").ap()

    q_spill = nc.dram_tensor("q_spill", [NH * (DN + DR), T], BF16, kind="Internal").ap()
    kr_spill = nc.dram_tensor("kr_spill", [NH * DR, T], BF16, kind="Internal").ap()
    attn_spill = nc.dram_tensor("attn_spill", [T, NH * DV], BF16, kind="Internal").ap()
    x2_spill = nc.dram_tensor("x2_spill", [T, D], F32, kind="Internal").ap()

    ident_np = np.eye(128, dtype=ml_dtypes.bfloat16)
    ident_h = nc.inline_tensor(ident_np, "identc").ap()
    trilq_np = np.where(np.arange(128)[None, :] >= np.arange(128)[:, None],
                        np.float32(0.0), np.float32(MASKNEG)).astype(np.float32)
    trilq_h = nc.inline_tensor(trilq_np, "trilqc").ap()
    ep_np = np.zeros((NE, NE * RK), dtype=ml_dtypes.bfloat16)
    for s in range(NE):
        ep_np[s, s * RK:(s + 1) * RK] = 1.0
    ep_h = nc.inline_tensor(ep_np, "epc").ap()

    tap_out = {}
    tap_shapes = {"ln1": ([128, 32768], BF16), "qlat": ([128, 24576], BF16),
                  "kvlat": ([128, 8192], BF16), "q_spill": ([NH * (DN + DR), T], BF16),
                  "kr": ([NH * DR, T], BF16), "attn": ([T, NH * DV], BF16),
                  "x2": ([T, D], F32)}
    for t_ in taps:
        sh, dt = tap_shapes[t_]
        tap_out[t_] = nc.dram_tensor("tap_" + t_, sh, dt, kind="ExternalOutput").ap()

    with contextlib.ExitStack() as es:
        AB = es.enter_context(nc.sbuf_tensor("AB", [128, NB], BF16))
        AF = es.enter_context(nc.sbuf_tensor("AF", [128, NF], F32))
        PS = es.enter_context(nc.psum_tensor("PS", [128, 4096], F32))
        p = Prog(nc, es)

        def B(off, w, p0=0, p1=128):
            return AB[p0:p1, off:off + w]

        def F(off, w, p0=0, p1=128):
            return AF[p0:p1, off:off + w]

        def PB(b, o=0, w=512, p0=0, p1=128):
            return PS[p0:p1, 512 * b + o: 512 * b + o + w]

        def ST(i):
            return AF[:, F_ST + i: F_ST + i + 1]

        p.dma("sync", B(O_ID, 128), ident_h, "dc")
        p.dma("sync", F(F_TRILQ, 128), trilq_h, "dc")
        p.dma("sync", B(O_EP, NE * RK, 0, NE), ep_h, "dc")
        p.dma("sync", F(F_QLN, 12), qln, "dc")
        d_consts = p.dma("sync", F(F_KVLN, 4), kvln, "dc")
        p.dma("gpsimd", B(O_ATTRB, NE, 0, NE), attrt, "dws")
        d_wab = p.dma("gpsimd", B(O_WAB, 1, 0, NE), wac, "dws")
        for k in range(16):
            p.dma("gpsimd", B(O_WBIG + 1536 * k, 1536), qa[128 * k:128 * (k + 1), :],
                  "dwa")
        d_qa = p.cnt["dwa"]
        dx0 = p.dma("sync", F(F_XB, 2048), x[0:128, :], "dxx")

        p.wait("tensor", "dws", d_wab)
        p.op("tensor", "matmul", PB(7, 0, 1, 0, NE), B(O_ATTRB, NE, 0, NE),
             B(O_WAB, 1, 0, NE), start=True, stop=True)
        t_aw = p.tpub()
        p.wait("scalar", "tn", t_aw)
        p.op("scalar", "activation", B(O_AW, 1, 0, NE), PB(7, 0, 1, 0, NE), ACT.Sigmoid,
             bias=0.0, scale=1.0)
        p.bank_rd_sa[7] = p.spub()

        def ln_chain(src_f32, dst_bf, eps):
            p.op("vector", "reduce_sum", ST(S_SSUM), src_f32, axis=AX)
            p.vdrain()
            p.op("vector", "tensor_scalar_mul", ST(S_M), ST(S_SSUM), 1.0 / D)
            p.vdrain()
            p.op("vector", "scalar_tensor_tensor", B(O_TRB, 2048), src_f32, ST(S_M),
                 src_f32, ALU.subtract, ALU.mult, accum_out=ST(S_S2))
            p.vdrain()
            p.op("vector", "tensor_scalar", ST(S_VARR), ST(S_S2), 1.0 / D, eps,
                 ALU.mult, ALU.add)
            v = p.vpub()
            p.wait("scalar", "vc", v)
            p.op("scalar", "activation", ST(S_T0), ST(S_VARR), ACT.Ln, bias=0.0, scale=1.0)
            sv = p.spub()
            p.wait("scalar", "sa", sv)
            p.op("scalar", "activation", ST(S_RSTD), ST(S_T0), ACT.Exp, bias=0.0, scale=-0.5)
            s = p.spub()
            p.wait("vector", "sa", s)
            p.op("vector", "tensor_scalar", dst_bf, src_f32, ST(S_M), ST(S_RSTD),
                 ALU.subtract, ALU.mult)
            return p.vpub()

        def transpose_tiles(src_off, ntile, dst_ap_fn, bA=3, bB=7):
            vlast = 0
            for i0 in range(0, ntile, 4):
                cnt = min(4, ntile - i0)
                bk = bA if (i0 // 4) % 2 == 0 else bB
                p.guard_banks([bk])
                for i in range(i0, i0 + cnt):
                    p.op("tensor", "matmul", PB(bk, 128 * (i - i0), 128),
                         B(src_off + 128 * i, 128), B(O_ID, 128), start=True, stop=True,
                         skip_group_check=True)
                tv = p.tpub()
                p.wait("vector", "tn", tv)
                p.op("vector", "tensor_copy", dst_ap_fn(i0, cnt), PB(bk, 0, 128 * cnt))
                vlast = p.vpub()
                p.bank_rd_vc[bk] = vlast
            return vlast

        def rms_chain(src_f32, n, dst_bf, eps):
            p.op("vector", "scalar_tensor_tensor", B(O_TRB, n), src_f32,
                 1.0, src_f32, ALU.mult, ALU.mult, accum_out=ST(S_SB))
            p.vdrain()
            p.op("vector", "tensor_scalar", ST(S_SC), ST(S_SB), 1.0 / n, eps,
                 ALU.mult, ALU.add)
            v = p.vpub()
            p.wait("scalar", "vc", v)
            p.op("scalar", "activation", ST(S_T1), ST(S_SC), ACT.Ln, bias=0.0, scale=1.0)
            sv = p.spub()
            p.wait("scalar", "sa", sv)
            p.op("scalar", "activation", ST(S_LNZ), ST(S_T1), ACT.Exp, bias=0.0, scale=-0.5)
            s = p.spub()
            p.wait("vector", "sa", s)
            p.op("vector", "tensor_scalar_mul", dst_bf, src_f32, ST(S_LNZ))
            return p.vpub()

        ln1_ready = [0] * NTC
        v_ln = [0] * NTC
        t_lntr = [0] * NTC
        dx_x = [0] * (NTC + 1)
        dx_x[0] = dx0
        p.wait("vector", "dc", d_consts)
        p.wait("vector", "dxx", dx_x[0])
        v_ln[0] = ln_chain(F(F_XB, 2048), B(O_LN1T, 2048), 1e-5)

        for c in range(NTC):
            def ln1_dst(i0, cnt, c=c):
                return strided_cols(AB, O_LN1 + 2048 * i0 + 128 * c, cnt, 2048, 128)
            p.wait("tensor", "vc", v_ln[c])
            ln1_ready[c] = transpose_tiles(O_LN1T, 16, ln1_dst)
            t_lntr[c] = p.cnt["tn"]
            p.wait("tensor", "dwa", d_qa)
            p.wait("tensor", "vc", ln1_ready[c])
            p.guard_banks([0, 1, 2])
            for k in range(16):
                lhs = B(O_LN1 + 2048 * k + 128 * c, 128)
                for b in range(3):
                    p.op("tensor", "matmul", PB(b), lhs,
                         B(O_WBIG + 1536 * k + 512 * b, 512),
                         start=(k == 0), stop=(k == 15), skip_group_check=True)
            t_qraw = p.tpub()
            if c + 1 < NTC:
                p.wait("sync", "vc", v_ln[c])
                dx_x[c + 1] = p.dma("sync", F(F_XB, 2048),
                                    x[128 * (c + 1):128 * (c + 2), :], "dxx")
                p.wait("vector", "dxx", dx_x[c + 1])
                p.wait("vector", "tn", t_lntr[c])
                v_ln[c + 1] = ln_chain(F(F_XB, 2048), B(O_LN1T, 2048), 1e-5)
            p.wait("vector", "tn", t_qraw)
            p.op("vector", "tensor_copy", AF[:, F_X2S:F_X2S + QL], PS[:, 0:QL])
            v_ev = p.vpub()
            for b in range(3):
                p.bank_rd_vc[b] = v_ev
            p.vdrain()
            rms_chain(AF[:, F_X2S:F_X2S + QL], QL, B(O_QLT, QL), 1e-6)
            v_qlt = p.cnt["vc"]

            def qlat_dst(i0, cnt, c=c):
                return strided_cols(AB, O_QLAT + 2048 * i0 + 128 * c, cnt, 2048, 128)
            p.wait("tensor", "vc", v_qlt)
            transpose_tiles(O_QLT, 12, qlat_dst)

        if "ln1" in taps:
            p.wait("sync", "vc", p.cnt["vc"])
            p.dma("sync", tap_out["ln1"], AB[:, O_LN1:O_LN1 + 32768], "dtap")
        if "qlat" in taps:
            p.wait("sync", "vc", p.cnt["vc"])
            p.dma("sync", tap_out["qlat"], AB[:, O_QLAT:O_QLAT + 24576], "dtap")

        p.wait("gpsimd", "tn", p.cnt["tn"])
        for k in range(16):
            p.dma("gpsimd", B(O_WBIG + 1536 * k, 1536), kva[128 * k:128 * (k + 1), :],
                  "dwa")
        d_kva = p.cnt["dwa"]

        kr_dma = 0

        def emit_kvmm(c):
            p.wait("tensor", "dwa", d_kva)
            p.guard_banks([0, 1, 2])
            for k in range(16):
                lhs = B(O_LN1 + 2048 * k + 128 * c, 128)
                for b in range(3):
                    p.op("tensor", "matmul", PB(b), lhs,
                         B(O_WBIG + 1536 * k + 512 * b, 512),
                         start=(k == 0), stop=(k == 15), skip_group_check=True)
            return p.tpub()

        t_kv = emit_kvmm(0)
        for c in range(NTC):
            p.wait("vector", "tn", t_kv)
            p.op("vector", "tensor_copy", B(O_TRB, 512), PB(1))
            p.op("vector", "tensor_copy", B(O_TRB + 512, 512), PB(2))
            p.op("vector", "tensor_copy", AF[:, F_X2S:F_X2S + KL], PB(0))
            v_ev = p.vpub()
            for b in range(3):
                p.bank_rd_vc[b] = v_ev
            if c + 1 < NTC:
                t_kv = emit_kvmm(c + 1)
            def kr_dst(i0, cnt):
                return B(O_LN1T + 128 * i0, 128 * cnt)
            p.wait("vector", "dkr", kr_dma)
            p.wait("tensor", "vc", v_ev)
            v_kr = transpose_tiles(O_TRB, 8, kr_dst)
            p.wait("sync", "vc", v_kr)
            src = B(O_LN1T, 1024).rearrange("p (f t) -> p f t", t=128)
            dst = kr_spill[0:1024, 128 * c:128 * (c + 1)].rearrange(
                "(f p) t -> p f t", p=128)
            kr_dma = p.dma("sync", dst, src, "dkr")
            p.vdrain()
            rms_chain(AF[:, F_X2S:F_X2S + KL], KL, B(O_QLT, 512), 1e-6)
            v_klt = p.cnt["vc"]

            def kvlat_dst(i0, cnt, c=c):
                return strided_cols(AB, O_KVLAT + 2048 * i0 + 128 * c, cnt, 2048, 128)
            p.wait("tensor", "vc", v_klt)
            transpose_tiles(O_QLT, 4, kvlat_dst)

        if "kvlat" in taps:
            p.wait("sync", "vc", p.cnt["vc"])
            p.dma("sync", tap_out["kvlat"], AB[:, O_KVLAT:O_KVLAT + 8192], "dtap")
        if "kr" in taps:
            p.wait("sync", "dkr", p.cnt["dkr"])
            p.dma("sync", tap_out["kr"], kr_spill, "dtap")

        p.wait("gpsimd", "tn", p.cnt["tn"])

        def qb_off(k):
            return (O_WBIG + 3072 * k) if k < 8 else (O_LN1 + 3072 * (k - 8))
        for k in range(12):
            p.dma("gpsimd", B(qb_off(k), 3072), qb[128 * k:128 * (k + 1), :], "dwa")
        p.wait("vector", "dwa", p.cnt["dwa"])
        for k in range(12):
            p.op("vector", "tensor_scalar_mul", B(qb_off(k), 3072), B(qb_off(k), 3072),
                 AF[:, F_QLN + k:F_QLN + k + 1])
        v_qb = p.vpub()
        p.wait("tensor", "vc", v_qb)

        stg = [O_LN1T, O_TRB]
        stg_free = [0, 0]
        p.wait("vector", "dkr", p.cnt["dkr"])
        for j in range(24):
            sb = j % 2
            p.wait("vector", "dqs0" if sb == 0 else "dqs1", stg_free[sb])
            for g in range(4):
                bk = g
                p.guard_banks([bk])
                for k in range(12):
                    p.op("tensor", "matmul", PB(bk), B(qb_off(k) + 128 * j, 128),
                         B(O_QLAT + 2048 * k + 512 * g, 512),
                         start=(k == 0), stop=(k == 11), skip_group_check=True)
                tv = p.tpub()
                p.wait("vector", "tn", tv)
                p.op("vector", "tensor_copy", B(stg[sb] + 512 * g, 512), PB(bk))
                p.bank_rd_vc[bk] = p.vpub()
            p.wait("sync", "vc", p.bank_rd_vc[3])
            stg_free[sb] = p.dma("sync", q_spill[128 * j:128 * (j + 1), :],
                                 B(stg[sb], 2048), "dqs0" if sb == 0 else "dqs1")

        if "q_spill" in taps:
            p.wait("sync", "dqs0", p.cnt["dqs0"])
            p.wait("sync", "dqs1", p.cnt["dqs1"])
            p.dma("sync", tap_out["q_spill"], q_spill, "dtap")

        p.wait("gpsimd", "tn", p.cnt["tn"])

        for k in range(4):
            p.dma("gpsimd", B(AT_KVBN + 2048 * k, 2048),
                  kvbn[128 * k:128 * (k + 1), :], "dwa")
        for k in range(4):
            p.dma("gpsimd", B(AT_KVBV + 2048 * k, 2048),
                  kvbv[128 * k:128 * (k + 1), :], "dwa")
        p.wait("vector", "dwa", p.cnt["dwa"])
        for k in range(4):
            p.op("vector", "tensor_scalar_mul", B(AT_KVBN + 2048 * k, 2048),
                 B(AT_KVBN + 2048 * k, 2048), AF[:, F_KVLN + k:F_KVLN + k + 1])
        for k in range(4):
            p.op("vector", "tensor_scalar_mul", B(AT_KVBV + 2048 * k, 2048),
                 B(AT_KVBV + 2048 * k, 2048), AF[:, F_KVLN + k:F_KVLN + k + 1])
        p.op("vector", "memset", B(O_ONE, 1), 1.0)
        for sl in range(2):
            p.op("vector", "memset", B(AT_QR + 2048 * sl, 2048, 64, 128), 0.0)
            p.op("vector", "memset", B(AT_KRH + 2048 * sl, 2048, 64, 128), 0.0)
        p.op("vector", "memset", B(O_MB + 8, 128, 0, 1), 1.0)
        p.op("vector", "memset", strided_cols(AB, AT_VH + 128, 16, 132, 1), 1.0)
        v_kvb = p.vpub()
        d_ow = 0
        for k in range(16):
            d_ow = p.dma("gpsimd", B(AT_OW + 2048 * k, 2048),
                         ow[128 * k:128 * (k + 1), :], "dwo")

        dq = [0, 0]
        p.wait("sync", "dqs0", p.cnt["dqs0"])
        p.wait("sync", "dqs1", p.cnt["dqs1"])
        p.wait("sync", "dkr", p.cnt["dkr"])
        p.dma("sync", B(AT_QN, 2048), q_spill[0:128, :], "dp0")
        p.dma("sync", B(AT_QR, 2048, 0, 64), q_spill[128:192, :], "dp0")
        dq[0] = p.dma("sync", B(AT_KRH, 2048, 0, 64), kr_spill[0:64, :], "dp0")

        p.wait("tensor", "vc", v_kvb)
        att_dma = 0
        ebuf_free = [0, 0]
        for h in range(NH):
            bufq = h % 2
            qn = AT_QN + 2048 * bufq
            qr = AT_QR + 2048 * bufq
            krh = AT_KRH + 2048 * bufq
            p.wait("tensor", "dp0" if bufq == 0 else "dp1", dq[bufq])
            p.guard_banks(list(range(8)))
            v_kn = 0
            for n in range(4):
                bk = n % 2
                p.guard_banks([bk])
                for kk in range(4):
                    p.op("tensor", "matmul", PB(bk), B(AT_KVBN + 2048 * kk + 128 * h, 128),
                         B(O_KVLAT + 2048 * kk + 512 * n, 512),
                         start=(kk == 0), stop=(kk == 3), skip_group_check=True)
                tv = p.tpub()
                p.wait("vector", "tn", tv)
                p.op("vector", "tensor_copy", B(AT_KNH + 512 * n, 512), PB(bk))
                v_kn = p.vpub()
                p.bank_rd_vc[bk] = v_kn
            v_vh = 0
            for tg in range(4):
                bk = tg % 2
                p.guard_banks([bk])
                for tc in range(4):
                    tt = 4 * tg + tc
                    for kk in range(4):
                        p.op("tensor", "matmul", PB(bk, 128 * tc, 128),
                             B(O_KVLAT + 2048 * kk + 128 * tt, 128),
                             B(AT_KVBV + 2048 * kk + 128 * h, 128),
                             start=(tc == 0 and kk == 0), stop=(tc == 3 and kk == 3),
                             skip_group_check=True)
                tv = p.tpub()
                p.wait("vector", "tn", tv)
                p.op("vector", "tensor_copy",
                     strided_cols(AB, AT_VH + 132 * 4 * tg, 4, 132, 128),
                     PB(bk, 0, 512))
                v_vh = p.vpub()
                p.bank_rd_vc[bk] = v_vh
            if h + 1 < NH:
                hn = h + 1
                nb = hn % 2
                p.wait("sync", "tn", p.cnt["tn"])
                psem = "dp0" if nb == 0 else "dp1"
                p.dma("sync", B(AT_QN + 2048 * nb, 2048),
                      q_spill[192 * hn:192 * hn + 128, :], psem)
                p.dma("sync", B(AT_QR + 2048 * nb, 2048, 0, 64),
                      q_spill[192 * hn + 128:192 * hn + 192, :], psem)
                dq[nb] = p.dma("sync", B(AT_KRH + 2048 * nb, 2048, 0, 64),
                               kr_spill[64 * hn:64 * hn + 64, :], psem)
            sqb = AT_PB
            sqb2 = AT_PB + 2048
            p.wait("vector", "dp0" if bufq == 0 else "dp1", dq[bufq])
            p.wait("vector", "tn", max(ebuf_free))

            def bound_half(nope_off, rope_off, out_slot):
                p.op("vector", "tensor_tensor", B(sqb, 2048), B(nope_off, 2048),
                     B(nope_off, 2048), ALU.mult)
                p.op("vector", "tensor_tensor", B(sqb2, 2048), B(rope_off, 2048),
                     B(rope_off, 2048), ALU.mult)
                v_sq = p.vpub()
                p.wait("tensor", "vc", v_sq)
                for n in range(4):
                    bk = n % 2
                    p.guard_banks([bk])
                    p.op("tensor", "matmul", PS[0:1, 512 * bk:512 * bk + 512],
                         B(O_ONE, 1), B(sqb + 512 * n, 512),
                         start=True, stop=False, skip_group_check=True)
                    p.op("tensor", "matmul", PS[0:1, 512 * bk:512 * bk + 512],
                         B(O_ONE, 1), B(sqb2 + 512 * n, 512),
                         start=False, stop=True, skip_group_check=True)
                    tvq = p.tpub()
                    p.wait("vector", "tn", tvq)
                    p.op("vector", "tensor_reduce",
                         AF[0:1, F_ESB + n:F_ESB + n + 1],
                         PS[0:1, 512 * bk:512 * bk + 512], AX, ALU.max)
                    p.bank_rd_vc[bk] = p.vpub()
                p.vdrain()
                p.op("vector", "tensor_max", AF[0:1, F_ESB + 4:F_ESB + 5],
                     AF[0:1, F_ESB:F_ESB + 1], AF[0:1, F_ESB + 1:F_ESB + 2])
                p.vdrain()
                p.op("vector", "tensor_max", AF[0:1, F_ESB + 5:F_ESB + 6],
                     AF[0:1, F_ESB + 4:F_ESB + 5], AF[0:1, F_ESB + 2:F_ESB + 3])
                p.vdrain()
                p.op("vector", "tensor_max", out_slot,
                     AF[0:1, F_ESB + 5:F_ESB + 6], AF[0:1, F_ESB + 3:F_ESB + 4])
                p.vdrain()

            bound_half(qn, qr, AF[0:1, F_ST + S_MX:F_ST + S_MX + 1])
            bound_half(AT_KNH, krh, AF[0:1, F_ST + S_Z:F_ST + S_Z + 1])
            v_km = p.vpub()
            t_bnd = p.cnt["tn"]
            ebuf_free[0] = max(ebuf_free[0], t_bnd)
            ebuf_free[1] = max(ebuf_free[1], t_bnd)
            p.wait("scalar", "vc", v_km)
            p.op("scalar", "activation", AF[0:1, F_ST + S_T0:F_ST + S_T0 + 1],
                 AF[0:1, F_ST + S_MX:F_ST + S_MX + 1], ACT.Ln, bias=0.0, scale=1.0)
            p.op("scalar", "activation", AF[0:1, F_ST + S_T1:F_ST + S_T1 + 1],
                 AF[0:1, F_ST + S_Z:F_ST + S_Z + 1], ACT.Ln, bias=0.0, scale=1.0)
            s_ln = p.spub()
            p.wait("vector", "sa", s_ln)
            p.op("vector", "tensor_add", AF[0:1, F_ST + S_B2:F_ST + S_B2 + 1],
                 AF[0:1, F_ST + S_T0:F_ST + S_T0 + 1],
                 AF[0:1, F_ST + S_T1:F_ST + S_T1 + 1])
            v_b = p.vpub()
            p.wait("scalar", "vc", v_b)
            p.op("scalar", "activation", AF[0:1, F_ST + S_REC:F_ST + S_REC + 1],
                 AF[0:1, F_ST + S_B2:F_ST + S_B2 + 1], ACT.Exp, bias=0.0, scale=0.5)
            s_mh = p.spub()
            p.wait("vector", "sa", s_mh)
            p.op("vector", "tensor_scalar_mul", B(O_MB, 1, 0, 1),
                 AF[0:1, F_ST + S_REC:F_ST + S_REC + 1], -SCALE)
            v_mb = p.vpub()
            p.wait("tensor", "vc", v_mb)
            p.guard_banks([0])
            p.op("tensor", "matmul", PB(0, 0, 1), B(O_MB + 8, 128, 0, 1),
                 B(O_MB, 1, 0, 1), start=True, stop=True, skip_group_check=True)
            t_bc = p.tpub()
            p.wait("vector", "tn", t_bc)
            p.op("vector", "tensor_copy", ST(S_NEGMS), PB(0, 0, 1))
            v_bias = p.vpub()
            p.bank_rd_vc[0] = v_bias
            p.wait("scalar", "vc", v_bias)
            def emit_pv(j, eb, qg0, s_exp):
                for i in range(j, NTC):
                    if i == j or (i % 4 == 0 and i // 4 > qg0):
                        p.wait("tensor", "sa", s_exp[i // 4])
                    bkk = 2 + i // 3
                    off = 129 * (i % 3)
                    p.op("tensor", "matmul",
                         PS[:, 512 * bkk + off: 512 * bkk + off + 129],
                         B(eb + 128 * i, 128), B(AT_VH + 132 * j, 129),
                         start=(j == 0 and i % 3 == 0), stop=False,
                         skip_group_check=True)
                ebuf_free[j % 2] = p.tpub()

            p.wait("tensor", "vc", v_vh)
            pv_args = None
            for j in range(NTC):
                eb = AT_PB + 2048 * (j % 2)
                qg0 = (128 * j) // 512
                s_exp = [0] * 4
                for qg in range(qg0, 4):
                    bk = qg % 2
                    p.guard_banks([bk])
                    p.op("tensor", "matmul", PB(bk), B(AT_KNH + 128 * j, 128),
                         B(qn + 512 * qg, 512), start=True, stop=False,
                         skip_group_check=True)
                    p.op("tensor", "matmul", PB(bk), B(krh + 128 * j, 128),
                         B(qr + 512 * qg, 512), start=False, stop=True,
                         skip_group_check=True)
                    t_st = p.tpub()
                    if qg == qg0:
                        p.wait("vector", "tn", t_st)
                        dg = PB(bk, 128 * (j % 4), 128)
                        p.op("vector", "tensor_add", dg, dg, F(F_TRILQ, 128))
                        v_mask = p.vpub()
                        p.bank_rd_vc[bk] = v_mask
                        p.wait("scalar", "vc", v_mask)
                    p.wait("scalar", "tn", max(t_st, ebuf_free[j % 2]))
                    p.op("scalar", "activation", B(eb + 512 * qg, 512), PB(bk),
                         ACT.Exp, bias=ST(S_NEGMS), scale=SCALE)
                    s_exp[qg] = p.spub()
                    p.bank_rd_sa[bk] = s_exp[qg]
                if pv_args is not None:
                    emit_pv(*pv_args)
                pv_args = (j, eb, qg0, list(s_exp))
            emit_pv(*pv_args)
            t_pv = ebuf_free[(NTC - 1) % 2]
            p.wait("vector", "tn", t_pv)
            p.wait("vector", "dat", att_dma)
            for b in range(6):
                cnt = 3 if b < 5 else 1
                srcz = PS[:, 512 * (2 + b): 512 * (2 + b) + 129 * cnt].rearrange(
                    "p (t s) -> p t s", s=129)[:, :, 128:129]
                dstz = AF[:, F_ESB + 3 * b: F_ESB + 3 * b + cnt].rearrange(
                    "p (t s) -> p t s", s=1)
                p.op("vector", "tensor_copy", dstz, srcz)
            v_zg = p.vpub()
            p.wait("scalar", "vc", v_zg)
            p.op("scalar", "activation", F(F_MASKF, 16), F(F_ESB, 16), ACT.Ln,
                 bias=0.0, scale=1.0)
            sv = p.spub()
            p.wait("scalar", "sa", sv)
            p.op("scalar", "activation", F(F_SC2, 16), F(F_MASKF, 16), ACT.Exp,
                 bias=0.0, scale=-1.0)
            s_rz = p.spub()
            p.wait("vector", "sa", s_rz)
            for i in range(NTC):
                bkk = 2 + i // 3
                off = 129 * (i % 3)
                p.op("vector", "tensor_scalar_mul", B(AT_ATS2 + 128 * i, 128),
                     PS[:, 512 * bkk + off: 512 * bkk + off + 128],
                     AF[:, F_SC2 + i: F_SC2 + i + 1])
            v_at = p.vpub()
            for bq in range(2, 8):
                p.bank_rd_vc[bq] = v_at
            p.wait("sync", "vc", v_at)
            dst = attn_spill[0:T, 128 * h:128 * (h + 1)].rearrange(
                "(i p) v -> p i v", p=128)
            src2 = B(AT_ATS2, 2048).rearrange("p (i v) -> p i v", v=128)
            att_dma = p.dma("sync", dst, src2, "dat")

        if "attn" in taps:
            p.wait("sync", "dat", p.cnt["dat"])
            p.dma("sync", tap_out["attn"], attn_spill, "dtap")

        p.wait("tensor", "dwo", d_ow)
        p.guard_banks(list(range(8)))
        dab = [0, 0]
        p.wait("sync", "dat", att_dma)
        p.dma("sync", B(AT_ABR, 2048), attn_spill[0:128, :], "db0")
        dab[0] = p.dma("sync", F(F_XB, 2048), x[0:128, :], "db0")
        v_add = 0
        d_x2 = 0
        for c in range(NTC):
            sb = c % 2
            p.wait("tensor", "db0" if sb == 0 else "db1", dab[sb])

            def abf_dst(i0, cnt, sb=sb):
                return B(AT_ABF + 2048 * sb + 128 * i0, 128 * cnt)
            v_abf = transpose_tiles(AT_ABR + 2048 * sb, 16, abf_dst, bA=5, bB=7)
            p.wait("tensor", "vc", v_abf)
            p.guard_banks([0, 1, 2, 3])
            for n in range(4):
                for k in range(16):
                    p.op("tensor", "matmul", PB(n),
                         B(AT_ABF + 2048 * sb + 128 * k, 128),
                         B(AT_OW + 2048 * k + 512 * n, 512),
                         start=(k == 0), stop=(k == 15), skip_group_check=True)
            t_o = p.tpub()
            if c + 1 < NTC:
                nb = (c + 1) % 2
                p.wait("sync", "tn", t_o)
                p.dma("sync", B(AT_ABR + 2048 * nb, 2048),
                      attn_spill[128 * (c + 1):128 * (c + 2), :],
                      "db0" if nb == 0 else "db1")
            p.wait("vector", "tn", t_o)
            p.wait("vector", "dxw", d_x2)
            for n in range(4):
                p.op("vector", "tensor_add", AF[:, F_X2S + 512 * n:F_X2S + 512 * n + 512],
                     PB(n), AF[:, F_XB + 512 * n:F_XB + 512 * n + 512])
            v_add = p.vpub()
            for n in range(4):
                p.bank_rd_vc[n] = v_add
            p.wait("sync", "vc", v_add)
            d_x2 = p.dma("sync", x2_spill[128 * c:128 * (c + 1), :], F(F_X2S, 2048), "dxw")
            if c + 1 < NTC:
                p.wait("sync", "vc", v_add)
                dab[(c + 1) % 2] = p.dma("sync", F(F_XB, 2048),
                                         x[128 * (c + 1):128 * (c + 2), :],
                                         "db0" if (c + 1) % 2 == 0 else "db1")
        if "x2" in taps:
            p.wait("sync", "dxw", p.cnt["dxw"])
            p.dma("sync", tap_out["x2"], x2_spill, "dtap")

        d_x2w = p.cnt["dxw"]
        p.wait("gpsimd", "tn", p.cnt["tn"])
        for k in range(16):
            p.dma("gpsimd", B(C_UA + 512 * k, 512), ua[128 * k:128 * (k + 1), :], "dwa")
        for k in range(4):
            p.dma("gpsimd", B(C_VA + 2048 * k, 2048), va[128 * k:128 * (k + 1), :], "dwa")
        for k in range(16):
            p.dma("gpsimd", B(C_WR + 32 * k, 32), wrt[128 * k:128 * (k + 1), :], "dwa")
        d_cw = p.cnt["dwa"]
        p.wait("sync", "dxw", d_x2w)
        dx2 = [0] * NTC
        d_outs = [0] * NTC
        t_l2tr = [0] * NTC
        t_hus = [0] * NTC

        def xslot(c):
            return F_XB if c % 2 == 0 else F_X2S

        def emit_lnhu(c):
            p.wait("vector", "dc0c" if c % 2 == 0 else "dc1c", dx2[c])
            if c >= 1:
                p.wait("vector", "tn", t_l2tr[c - 1])
            v_ln2 = ln_chain(AF[:, xslot(c):xslot(c) + 2048], B(C_LN2T, 2048), 1e-5)
            p.wait("tensor", "vc", v_ln2)

            def ln2_dst(i0, cnt):
                return B(C_LN2F + 128 * i0, 128 * cnt)
            v_l2f = transpose_tiles(C_LN2T, 16, ln2_dst, bA=5, bB=7)
            t_l2tr[c] = p.cnt["tn"]
            p.wait("tensor", "vc", v_l2f)
            p.guard_banks([4, 6])
            for k in range(16):
                lhs = B(C_LN2F + 128 * k, 128)
                p.op("tensor", "matmul", PB(4), lhs, B(C_UA + 512 * k, 512),
                     start=(k == 0), stop=(k == 15), skip_group_check=True)
                p.op("tensor", "matmul", PB(6, 0, 32), lhs, B(C_WR + 32 * k, 32),
                     start=(k == 0), stop=(k == 15), skip_group_check=True)
            t_hus[c] = p.tpub()

        dx2[0] = p.dma("sync", F(F_XB, 2048), x2_spill[0:128, :], "dc0c")
        p.wait("tensor", "dwa", d_cw)
        emit_lnhu(0)
        for c in range(NTC):
            p.wait("vector", "tn", t_hus[c])
            scp = PB(6, 0, 32)
            p.op("vector", "tensor_reduce", ST(S_M1), scp, AX, ALU.max)
            p.vdrain()
            p.op("vector", "tensor_scalar_mul", ST(S_NEGM1), ST(S_M1), -1.0)
            p.op("vector", "tensor_single_scalar", F(F_MASKF, 32), scp, ST(S_M1),
                 ALU.is_equal)
            p.vdrain()
            p.op("vector", "scalar_tensor_tensor", F(F_SC2, 32), F(F_MASKF, 32),
                 MASKNEG, scp, ALU.mult, ALU.add)
            p.vdrain()
            p.op("vector", "tensor_reduce", ST(S_M2), F(F_SC2, 32), AX, ALU.max)
            v_r1 = p.vpub()
            p.wait("scalar", "vc", v_r1)
            p.op("scalar", "activation", F(F_ESB, 32), scp, ACT.Exp,
                 bias=ST(S_NEGM1), scale=1.0, accum_out=ST(S_ZR))
            s_e = p.spub()
            p.bank_rd_sa[6] = s_e
            p.wait("vector", "sa", s_e)
            p.op("vector", "tensor_single_scalar", F(F_MASKF, 32), scp, ST(S_M2),
                 ALU.is_ge)
            p.vdrain()
            p.op("vector", "scalar_tensor_tensor", F(F_MASKF, 32), F(F_ESB, 32), 1.0,
                 F(F_MASKF, 32), ALU.mult, ALU.mult, accum_out=ST(S_S2G))
            p.vdrain()
            p.op("vector", "tensor_scalar", ST(S_DEN), ST(S_ZR), 1e-8, ST(S_S2G),
                 ALU.mult, ALU.add)
            v_den = p.vpub()
            p.wait("scalar", "vc", v_den)
            p.op("scalar", "activation", ST(S_T0), ST(S_DEN), ACT.Ln, bias=0.0,
                 scale=1.0)
            sv2 = p.spub()
            p.wait("scalar", "sa", sv2)
            p.op("scalar", "activation", ST(S_REC), ST(S_T0), ACT.Exp, bias=0.0,
                 scale=-1.0)
            s_rec = p.spub()
            p.wait("vector", "sa", s_rec)
            p.op("vector", "tensor_scalar_mul", B(C_G, 32), F(F_MASKF, 32), ST(S_REC))
            v_g = p.vpub()
            p.bank_rd_vc[6] = v_g
            p.wait("tensor", "vc", v_g)
            p.guard_banks([7])
            p.op("tensor", "matmul", PB(7, 0, 128, 0, NE), B(C_G, 32), B(O_ID, 128),
                 start=True, stop=True, skip_group_check=True)
            t_g = p.tpub()
            p.wait("vector", "tn", t_g)
            p.op("vector", "tensor_copy", B(C_GF, 128, 0, NE), PB(7, 0, 128, 0, NE))
            v_gf = p.vpub()
            p.bank_rd_vc[7] = v_gf
            p.wait("tensor", "vc", v_gf)
            p.guard_banks([5])
            p.op("tensor", "matmul", PB(5), B(C_GF, 128, 0, NE), B(O_EP, 512, 0, NE),
                 start=True, stop=True, skip_group_check=True)
            p.op("tensor", "matmul", PB(7, 128, 1), B(C_GF, 128, 0, NE),
                 B(O_AW, 1, 0, NE), start=True, stop=True, skip_group_check=True)
            t_ge = p.tpub()
            p.wait("vector", "tn", t_ge)
            p.op("vector", "tensor_copy", B(C_GE, 512), PB(5))
            p.vdrain()
            p.op("vector", "tensor_tensor", B(C_ZSB, 512), PB(4), B(C_GE, 512), ALU.mult)
            p.vdrain()
            p.op("vector", "tensor_scalar", ST(S_F), PB(7, 128, 1), 0.2, 0.9,
                 ALU.mult, ALU.add)
            v_z = p.vpub()
            p.bank_rd_vc[4] = v_z
            p.bank_rd_vc[5] = v_z
            p.bank_rd_vc[7] = v_z
            p.wait("tensor", "vc", v_z)
            p.guard_banks([6])
            for kz in range(4):
                p.op("tensor", "matmul", PB(6, 128 * kz, 128), B(C_ZSB + 128 * kz, 128),
                     B(O_ID, 128), start=True, stop=True, skip_group_check=True)
            t_zt = p.tpub()
            p.wait("vector", "tn", t_zt)
            p.op("vector", "tensor_copy", B(C_ZF, 512), PB(6))
            v_zf = p.vpub()
            p.bank_rd_vc[6] = v_zf
            if c + 1 < NTC:
                if c >= 1:
                    p.wait("sync", "dout", d_outs[c - 1])
                dx2[c + 1] = p.dma("sync", AF[:, xslot(c + 1):xslot(c + 1) + 2048],
                                   x2_spill[128 * (c + 1):128 * (c + 2), :],
                                   "dc0c" if (c + 1) % 2 == 0 else "dc1c")
                emit_lnhu(c + 1)
            p.wait("tensor", "vc", v_zf)
            p.guard_banks([0, 1, 2, 3])
            for n in range(4):
                for k in range(4):
                    p.op("tensor", "matmul", PB(n), B(C_ZF + 128 * k, 128),
                         B(C_VA + 2048 * k + 512 * n, 512),
                         start=(k == 0), stop=(k == 3), skip_group_check=True)
            t_f = p.tpub()
            p.wait("vector", "tn", t_f)
            for n in range(4):
                sl = xslot(c) + 512 * n
                p.op("vector", "scalar_tensor_tensor",
                     AF[:, sl:sl + 512], PB(n), ST(S_F),
                     AF[:, sl:sl + 512], ALU.mult, ALU.add)
            v_out = p.vpub()
            for n in range(4):
                p.bank_rd_vc[n] = v_out
            p.wait("sync", "vc", v_out)
            d_outs[c] = p.dma("sync", out[128 * c:128 * (c + 1), :],
                              AF[:, xslot(c):xslot(c) + 2048], "dout")

        for s in p.dma_sems:
            p.wait("sync", s, p.cnt[s])

        with nc.Block() as block:
            @block.sync
            def _(e):
                p.replay("sync", e)

            @block.gpsimd
            def _(e):
                p.replay("gpsimd", e)

            @block.tensor
            def _(e):
                p.replay("tensor", e)

            @block.vector
            def _(e):
                p.replay("vector", e)

            @block.scalar
            def _(e):
                p.replay("scalar", e)

    return nc


def prep_inputs(inputs):
    f32 = np.float32
    qa = np.ascontiguousarray(inputs["L0_attn_q_a_w"], dtype=f32)
    kva = np.ascontiguousarray(inputs["L0_attn_kv_a_w"], dtype=f32)
    qb = np.ascontiguousarray(inputs["L0_attn_q_b_w"], dtype=f32)
    kvb = np.asarray(inputs["L0_attn_kv_b_w"], dtype=f32)
    kvbn = np.ascontiguousarray(kvb.reshape(KL, NH, DN + DV)[:, :, :DN].reshape(KL, NH * DN))
    kvbv = np.ascontiguousarray(kvb.reshape(KL, NH, DN + DV)[:, :, DN:].reshape(KL, NH * DV))
    ow = np.ascontiguousarray(inputs["L0_attn_o_w"], dtype=f32)
    wrt = np.ascontiguousarray(np.asarray(inputs["L0_ffn_Wr"], dtype=f32).T)
    ua = np.ascontiguousarray(np.asarray(inputs["L0_ffn_U"], dtype=f32).transpose(1, 0, 2).reshape(D, NE * RK))
    va = np.ascontiguousarray(np.asarray(inputs["L0_ffn_V"], dtype=f32).reshape(NE * RK, D))
    attrt = np.ascontiguousarray(np.asarray(inputs["L0_ffn_attr"], dtype=f32).T)
    wac = np.ascontiguousarray(np.asarray(inputs["L0_ffn_Wa"], dtype=f32)[0][:, None])
    qln = np.ascontiguousarray(np.asarray(inputs["L0_attn_q_a_ln_w"], dtype=f32).reshape(12, 128).T)
    kvln = np.ascontiguousarray(np.asarray(inputs["L0_attn_kv_a_ln_w"], dtype=f32).reshape(4, 128).T)
    return dict(qa=qa, kva=kva, qb=qb, kvbn=kvbn, kvbv=kvbv, ow=ow, wrt=wrt,
                ua=ua, va=va, attrt=attrt, wac=wac, qln=qln, kvln=kvln)


def kernel(**inputs):
    shared = prep_inputs(inputs)
    x = np.ascontiguousarray(np.asarray(inputs["L0_attn_hidden_states"], dtype=np.float32))
    nc = build()
    in_maps = []
    for c in range(8):
        m = dict(shared)
        m["x"] = np.ascontiguousarray(x[c])
        in_maps.append(m)
    res = run_bass_kernel_spmd(nc, in_maps, core_ids=list(range(8)))
    return np.stack([res.results[c]["out"] for c in range(8)], axis=0)


# version 21
# speedup vs baseline: 1.1208x; 1.0315x over previous
import math
import contextlib
import numpy as np
import ml_dtypes
import concourse.bass as bass
import concourse.mybir as mybir
from concourse.bass_utils import run_bass_kernel_spmd

F32 = mybir.dt.float32
BF16 = mybir.dt.bfloat16
AX = mybir.AxisListType.X
ALU = mybir.AluOpType
ACT = mybir.ActivationFunctionType

T = 2048
D = 2048
NH = 16
DN = 128
DR = 64
DV = 128
QL = 1536
KL = 512
NE = 32
RK = 16
NTC = T // 128
SCALE = float(np.float32(1.0) / np.sqrt(np.float32(DN + DR)))
MASKNEG = -1.0e9

O_ID = 0
O_EP = 128
O_AW = 640
O_ATTRB = 656
O_WAB = 688
O_ONE = 704
O_MB = 720
O_T01 = 880
O_LN1 = 1024
O_QLAT = 33792
O_WBIG = 58368
O_LN1T = 82944
O_QLT = 84992
O_TRB = 86528
O_KVLAT = 88576
NB = 96768
AT_KVBN = 1024
AT_KVBV = 9216
AT_OW = 17408
AT_QN = 50176
AT_QR = 54272
AT_KNH = 58368
AT_KRH = 60416
AT_VH = 64512
AT_PB = 66624
AT_ATS2 = 70720
AT_ABR = 72768
AT_ABF = 76864
C_UA = 1024
C_VA = 9216
C_WR = 17408
C_LN2T = 17920
C_LN2F = 19968
C_ZSB = 22016
C_ZF = 22528
C_GF = 23040
C_G = 23168
C_GE = 23296

F_TRILQ = 0
F_QLN = 128
F_KVLN = 140
F_ST = 144
F_ESB = F_ST + 48
F_MASKF = F_ST + 80
F_SC2 = F_ST + 112
F_XB = 304
F_X2S = 2352
NF = 4400

(S_SSUM, S_M, S_S2, S_VARR, S_RSTD, S_MX, S_NEGMS, S_Z, S_LNZ, S_B2,
 S_M1, S_NEGM1, S_ZR, S_M2, S_S2G, S_DEN, S_REC, S_F, S_SA, S_SB, S_SC,
 S_T0, S_T1) = range(23)

ENGINES = ("sync", "gpsimd", "tensor", "vector", "scalar")


class Prog:
    def __init__(self, nc, es):
        self.nc = nc
        self.sched = {e: [] for e in ENGINES}
        self.semh = {}
        self.cnt = {}
        self.dma_sems = ("dc", "dws", "dwa", "dwo", "dxx", "dkr", "dqs0", "dqs1",
                         "dp0", "dp1", "dat", "db0", "db1", "dxw", "dc0c", "dc1c",
                         "dout", "dtap")
        for s in ("tn", "vc", "sa") + self.dma_sems:
            self.semh[s] = es.enter_context(nc.semaphore(s))
            self.cnt[s] = 0
        self.bank_rd_vc = [0] * 8
        self.bank_rd_sa = [0] * 8

    def op(self, eng, name, *args, inc=None, **kw):
        self.sched[eng].append([name, args, kw, inc])

    def wait(self, eng, sem, val):
        if val <= 0:
            return
        self.sched[eng].append(["__wait__", (sem, val), {}, None])

    def bump(self, sem, n):
        self.cnt[sem] += n
        return self.cnt[sem]

    def dma(self, eng, dst, src, sem):
        v = self.bump(sem, 16)
        self.op(eng, "dma_start", dst, src, inc=(sem, 16))
        return v

    def _pub(self, eng, sem):
        v = self.bump(sem, 1)
        last = self.sched[eng][-1] if self.sched[eng] else None
        if last is not None and last[0] != "__wait__" and last[3] is None:
            last[3] = (sem, 1)
        else:
            self.op(eng, "drain", inc=(sem, 1))
        return v

    def vpub(self):
        return self._pub("vector", "vc")

    def vdrain(self):
        self.op("vector", "drain")

    def spub(self):
        return self._pub("scalar", "sa")

    def tpub(self):
        return self._pub("tensor", "tn")

    def replay(self, eng, e):
        semh = self.semh
        for name, args, kw, inc in self.sched[eng]:
            if name == "__wait__":
                e.wait_ge(semh[args[0]], args[1])
                continue
            ins = getattr(e, name)(*args, **kw)
            if inc is not None:
                ins.then_inc(semh[inc[0]], inc[1])

    def barrier(self, engines=("tensor", "vector", "scalar")):
        for eng in engines:
            for sem in ("tn", "vc", "sa") + self.dma_sems:
                self.wait(eng, sem, self.cnt[sem])

    def guard_banks(self, banks):
        v = max(self.bank_rd_vc[b] for b in banks)
        s = max(self.bank_rd_sa[b] for b in banks)
        self.wait("tensor", "vc", v)
        self.wait("tensor", "sa", s)


def strided_cols(arena, base, n, stride, w, limit=NB):
    if base + n * stride <= limit:
        big = arena[:, base: base + n * stride]
        return big.rearrange("p (f t) -> p f t", t=stride)[:, :, 0:w]
    pad = stride - w
    big = arena[:, base - pad: base - pad + n * stride]
    return big.rearrange("p (f t) -> p f t", t=stride)[:, :, pad:stride]


def build(taps=()):
    nc = bass.Bass("TRN2", target_bir_lowering=False, debug=False)

    x = nc.dram_tensor("x", [T, D], F32, kind="ExternalInput").ap()
    qa = nc.dram_tensor("qa", [D, QL], F32, kind="ExternalInput").ap()
    kva = nc.dram_tensor("kva", [D, QL], F32, kind="ExternalInput").ap()
    qb = nc.dram_tensor("qb", [QL, NH * (DN + DR)], F32, kind="ExternalInput").ap()
    kvbn = nc.dram_tensor("kvbn", [KL, NH * DN], F32, kind="ExternalInput").ap()
    kvbv = nc.dram_tensor("kvbv", [KL, NH * DV], F32, kind="ExternalInput").ap()
    ow = nc.dram_tensor("ow", [NH * DV, D], F32, kind="ExternalInput").ap()
    wrt = nc.dram_tensor("wrt", [D, NE], F32, kind="ExternalInput").ap()
    ua = nc.dram_tensor("ua", [D, NE * RK], F32, kind="ExternalInput").ap()
    va = nc.dram_tensor("va", [NE * RK, D], F32, kind="ExternalInput").ap()
    attrt = nc.dram_tensor("attrt", [NE, NE], F32, kind="ExternalInput").ap()
    wac = nc.dram_tensor("wac", [NE, 1], F32, kind="ExternalInput").ap()
    qln = nc.dram_tensor("qln", [128, 12], F32, kind="ExternalInput").ap()
    kvln = nc.dram_tensor("kvln", [128, 4], F32, kind="ExternalInput").ap()
    out = nc.dram_tensor("out", [T, D], F32, kind="ExternalOutput").ap()

    q_spill = nc.dram_tensor("q_spill", [NH * (DN + DR), T], BF16, kind="Internal").ap()
    kr_spill = nc.dram_tensor("kr_spill", [NH * DR, T], BF16, kind="Internal").ap()
    attn_spill = nc.dram_tensor("attn_spill", [T, NH * DV], BF16, kind="Internal").ap()
    x2_spill = nc.dram_tensor("x2_spill", [T, D], F32, kind="Internal").ap()

    ident_np = np.eye(128, dtype=ml_dtypes.bfloat16)
    ident_h = nc.inline_tensor(ident_np, "identc").ap()
    trilq_np = np.where(np.arange(128)[None, :] >= np.arange(128)[:, None],
                        np.float32(0.0), np.float32(MASKNEG)).astype(np.float32)
    trilq_h = nc.inline_tensor(trilq_np, "trilqc").ap()
    tril01_np = np.where(np.arange(128)[None, :] >= np.arange(128)[:, None],
                         1.0, 0.0).astype(ml_dtypes.bfloat16)
    tril01_h = nc.inline_tensor(tril01_np, "tril01c").ap()
    ep_np = np.zeros((NE, NE * RK), dtype=ml_dtypes.bfloat16)
    for s in range(NE):
        ep_np[s, s * RK:(s + 1) * RK] = 1.0
    ep_h = nc.inline_tensor(ep_np, "epc").ap()

    tap_out = {}
    tap_shapes = {"ln1": ([128, 32768], BF16), "qlat": ([128, 24576], BF16),
                  "kvlat": ([128, 8192], BF16), "q_spill": ([NH * (DN + DR), T], BF16),
                  "kr": ([NH * DR, T], BF16), "attn": ([T, NH * DV], BF16),
                  "x2": ([T, D], F32)}
    for t_ in taps:
        sh, dt = tap_shapes[t_]
        tap_out[t_] = nc.dram_tensor("tap_" + t_, sh, dt, kind="ExternalOutput").ap()

    with contextlib.ExitStack() as es:
        AB = es.enter_context(nc.sbuf_tensor("AB", [128, NB], BF16))
        AF = es.enter_context(nc.sbuf_tensor("AF", [128, NF], F32))
        PS = es.enter_context(nc.psum_tensor("PS", [128, 4096], F32))
        p = Prog(nc, es)

        def B(off, w, p0=0, p1=128):
            return AB[p0:p1, off:off + w]

        def F(off, w, p0=0, p1=128):
            return AF[p0:p1, off:off + w]

        def PB(b, o=0, w=512, p0=0, p1=128):
            return PS[p0:p1, 512 * b + o: 512 * b + o + w]

        def ST(i):
            return AF[:, F_ST + i: F_ST + i + 1]

        p.dma("sync", B(O_ID, 128), ident_h, "dc")
        p.dma("sync", F(F_TRILQ, 128), trilq_h, "dc")
        p.dma("sync", B(O_EP, NE * RK, 0, NE), ep_h, "dc")
        p.dma("sync", F(F_QLN, 12), qln, "dc")
        p.dma("sync", B(O_T01, 128), tril01_h, "dc")
        d_consts = p.dma("sync", F(F_KVLN, 4), kvln, "dc")
        p.dma("gpsimd", B(O_ATTRB, NE, 0, NE), attrt, "dws")
        d_wab = p.dma("gpsimd", B(O_WAB, 1, 0, NE), wac, "dws")
        for k in range(16):
            p.dma("gpsimd", B(O_WBIG + 1536 * k, 1536), qa[128 * k:128 * (k + 1), :],
                  "dwa")
        d_qa = p.cnt["dwa"]
        dx0 = p.dma("sync", F(F_XB, 2048), x[0:128, :], "dxx")

        p.wait("tensor", "dws", d_wab)
        p.op("tensor", "matmul", PB(7, 0, 1, 0, NE), B(O_ATTRB, NE, 0, NE),
             B(O_WAB, 1, 0, NE), start=True, stop=True)
        t_aw = p.tpub()
        p.wait("scalar", "tn", t_aw)
        p.op("scalar", "activation", B(O_AW, 1, 0, NE), PB(7, 0, 1, 0, NE), ACT.Sigmoid,
             bias=0.0, scale=1.0)
        p.bank_rd_sa[7] = p.spub()

        def ln_chain(src_f32, dst_bf, eps):
            p.op("vector", "reduce_sum", ST(S_SSUM), src_f32, axis=AX)
            p.vdrain()
            p.op("vector", "tensor_scalar_mul", ST(S_M), ST(S_SSUM), 1.0 / D)
            p.vdrain()
            p.op("vector", "scalar_tensor_tensor", B(O_TRB, 2048), src_f32, ST(S_M),
                 src_f32, ALU.subtract, ALU.mult, accum_out=ST(S_S2))
            p.vdrain()
            p.op("vector", "tensor_scalar", ST(S_VARR), ST(S_S2), 1.0 / D, eps,
                 ALU.mult, ALU.add)
            v = p.vpub()
            p.wait("scalar", "vc", v)
            p.op("scalar", "activation", ST(S_T0), ST(S_VARR), ACT.Ln, bias=0.0, scale=1.0)
            sv = p.spub()
            p.wait("scalar", "sa", sv)
            p.op("scalar", "activation", ST(S_RSTD), ST(S_T0), ACT.Exp, bias=0.0, scale=-0.5)
            s = p.spub()
            p.wait("vector", "sa", s)
            p.op("vector", "tensor_scalar", dst_bf, src_f32, ST(S_M), ST(S_RSTD),
                 ALU.subtract, ALU.mult)
            return p.vpub()

        def transpose_tiles(src_off, ntile, dst_ap_fn, bA=3, bB=7):
            vlast = 0
            for i0 in range(0, ntile, 4):
                cnt = min(4, ntile - i0)
                bk = bA if (i0 // 4) % 2 == 0 else bB
                p.guard_banks([bk])
                for i in range(i0, i0 + cnt):
                    p.op("tensor", "matmul", PB(bk, 128 * (i - i0), 128),
                         B(src_off + 128 * i, 128), B(O_ID, 128), start=True, stop=True,
                         skip_group_check=True)
                tv = p.tpub()
                p.wait("vector", "tn", tv)
                p.op("vector", "tensor_copy", dst_ap_fn(i0, cnt), PB(bk, 0, 128 * cnt))
                vlast = p.vpub()
                p.bank_rd_vc[bk] = vlast
            return vlast

        def rms_chain(src_f32, n, dst_bf, eps):
            p.op("vector", "scalar_tensor_tensor", B(O_TRB, n), src_f32,
                 1.0, src_f32, ALU.mult, ALU.mult, accum_out=ST(S_SB))
            p.vdrain()
            p.op("vector", "tensor_scalar", ST(S_SC), ST(S_SB), 1.0 / n, eps,
                 ALU.mult, ALU.add)
            v = p.vpub()
            p.wait("scalar", "vc", v)
            p.op("scalar", "activation", ST(S_T1), ST(S_SC), ACT.Ln, bias=0.0, scale=1.0)
            sv = p.spub()
            p.wait("scalar", "sa", sv)
            p.op("scalar", "activation", ST(S_LNZ), ST(S_T1), ACT.Exp, bias=0.0, scale=-0.5)
            s = p.spub()
            p.wait("vector", "sa", s)
            p.op("vector", "tensor_scalar_mul", dst_bf, src_f32, ST(S_LNZ))
            return p.vpub()

        ln1_ready = [0] * NTC
        v_ln = [0] * NTC
        t_lntr = [0] * NTC
        dx_x = [0] * (NTC + 1)
        dx_x[0] = dx0
        p.wait("vector", "dc", d_consts)
        p.wait("vector", "dxx", dx_x[0])
        v_ln[0] = ln_chain(F(F_XB, 2048), B(O_LN1T, 2048), 1e-5)

        for c in range(NTC):
            def ln1_dst(i0, cnt, c=c):
                return strided_cols(AB, O_LN1 + 2048 * i0 + 128 * c, cnt, 2048, 128)
            p.wait("tensor", "vc", v_ln[c])
            ln1_ready[c] = transpose_tiles(O_LN1T, 16, ln1_dst)
            t_lntr[c] = p.cnt["tn"]
            p.wait("tensor", "dwa", d_qa)
            p.wait("tensor", "vc", ln1_ready[c])
            p.guard_banks([0, 1, 2])
            for k in range(16):
                lhs = B(O_LN1 + 2048 * k + 128 * c, 128)
                for b in range(3):
                    p.op("tensor", "matmul", PB(b), lhs,
                         B(O_WBIG + 1536 * k + 512 * b, 512),
                         start=(k == 0), stop=(k == 15), skip_group_check=True)
            t_qraw = p.tpub()
            if c + 1 < NTC:
                p.wait("sync", "vc", v_ln[c])
                dx_x[c + 1] = p.dma("sync", F(F_XB, 2048),
                                    x[128 * (c + 1):128 * (c + 2), :], "dxx")
                p.wait("vector", "dxx", dx_x[c + 1])
                p.wait("vector", "tn", t_lntr[c])
                v_ln[c + 1] = ln_chain(F(F_XB, 2048), B(O_LN1T, 2048), 1e-5)
            p.wait("vector", "tn", t_qraw)
            p.op("vector", "tensor_copy", AF[:, F_X2S:F_X2S + QL], PS[:, 0:QL])
            v_ev = p.vpub()
            for b in range(3):
                p.bank_rd_vc[b] = v_ev
            p.vdrain()
            rms_chain(AF[:, F_X2S:F_X2S + QL], QL, B(O_QLT, QL), 1e-6)
            v_qlt = p.cnt["vc"]

            def qlat_dst(i0, cnt, c=c):
                return strided_cols(AB, O_QLAT + 2048 * i0 + 128 * c, cnt, 2048, 128)
            p.wait("tensor", "vc", v_qlt)
            transpose_tiles(O_QLT, 12, qlat_dst)

        if "ln1" in taps:
            p.wait("sync", "vc", p.cnt["vc"])
            p.dma("sync", tap_out["ln1"], AB[:, O_LN1:O_LN1 + 32768], "dtap")
        if "qlat" in taps:
            p.wait("sync", "vc", p.cnt["vc"])
            p.dma("sync", tap_out["qlat"], AB[:, O_QLAT:O_QLAT + 24576], "dtap")

        p.wait("gpsimd", "tn", p.cnt["tn"])
        for k in range(16):
            p.dma("gpsimd", B(O_WBIG + 1536 * k, 1536), kva[128 * k:128 * (k + 1), :],
                  "dwa")
        d_kva = p.cnt["dwa"]

        kr_dma = 0

        def emit_kvmm(c):
            p.wait("tensor", "dwa", d_kva)
            p.guard_banks([0, 1, 2])
            for k in range(16):
                lhs = B(O_LN1 + 2048 * k + 128 * c, 128)
                for b in range(3):
                    p.op("tensor", "matmul", PB(b), lhs,
                         B(O_WBIG + 1536 * k + 512 * b, 512),
                         start=(k == 0), stop=(k == 15), skip_group_check=True)
            return p.tpub()

        t_kv = emit_kvmm(0)
        for c in range(NTC):
            p.wait("vector", "tn", t_kv)
            p.op("vector", "tensor_copy", B(O_TRB, 512), PB(1))
            p.op("vector", "tensor_copy", B(O_TRB + 512, 512), PB(2))
            p.op("vector", "tensor_copy", AF[:, F_X2S:F_X2S + KL], PB(0))
            v_ev = p.vpub()
            for b in range(3):
                p.bank_rd_vc[b] = v_ev
            if c + 1 < NTC:
                t_kv = emit_kvmm(c + 1)
            def kr_dst(i0, cnt):
                return B(O_LN1T + 128 * i0, 128 * cnt)
            p.wait("vector", "dkr", kr_dma)
            p.wait("tensor", "vc", v_ev)
            v_kr = transpose_tiles(O_TRB, 8, kr_dst)
            p.wait("sync", "vc", v_kr)
            src = B(O_LN1T, 1024).rearrange("p (f t) -> p f t", t=128)
            dst = kr_spill[0:1024, 128 * c:128 * (c + 1)].rearrange(
                "(f p) t -> p f t", p=128)
            kr_dma = p.dma("sync", dst, src, "dkr")
            p.vdrain()
            rms_chain(AF[:, F_X2S:F_X2S + KL], KL, B(O_QLT, 512), 1e-6)
            v_klt = p.cnt["vc"]

            def kvlat_dst(i0, cnt, c=c):
                return strided_cols(AB, O_KVLAT + 2048 * i0 + 128 * c, cnt, 2048, 128)
            p.wait("tensor", "vc", v_klt)
            transpose_tiles(O_QLT, 4, kvlat_dst)

        if "kvlat" in taps:
            p.wait("sync", "vc", p.cnt["vc"])
            p.dma("sync", tap_out["kvlat"], AB[:, O_KVLAT:O_KVLAT + 8192], "dtap")
        if "kr" in taps:
            p.wait("sync", "dkr", p.cnt["dkr"])
            p.dma("sync", tap_out["kr"], kr_spill, "dtap")

        p.wait("gpsimd", "tn", p.cnt["tn"])

        def qb_off(k):
            return (O_WBIG + 3072 * k) if k < 8 else (O_LN1 + 3072 * (k - 8))
        for k in range(12):
            p.dma("gpsimd", B(qb_off(k), 3072), qb[128 * k:128 * (k + 1), :], "dwa")
        p.wait("vector", "dwa", p.cnt["dwa"])
        for k in range(12):
            p.op("vector", "tensor_scalar_mul", B(qb_off(k), 3072), B(qb_off(k), 3072),
                 AF[:, F_QLN + k:F_QLN + k + 1])
        v_qb = p.vpub()
        p.wait("tensor", "vc", v_qb)

        stg = [O_LN1T, O_TRB]
        stg_free = [0, 0]
        p.wait("vector", "dkr", p.cnt["dkr"])
        for j in range(24):
            sb = j % 2
            p.wait("vector", "dqs0" if sb == 0 else "dqs1", stg_free[sb])
            for g in range(4):
                bk = g
                p.guard_banks([bk])
                for k in range(12):
                    p.op("tensor", "matmul", PB(bk), B(qb_off(k) + 128 * j, 128),
                         B(O_QLAT + 2048 * k + 512 * g, 512),
                         start=(k == 0), stop=(k == 11), skip_group_check=True)
                tv = p.tpub()
                p.wait("vector", "tn", tv)
                p.op("vector", "tensor_copy", B(stg[sb] + 512 * g, 512), PB(bk))
                p.bank_rd_vc[bk] = p.vpub()
            p.wait("sync", "vc", p.bank_rd_vc[3])
            stg_free[sb] = p.dma("sync", q_spill[128 * j:128 * (j + 1), :],
                                 B(stg[sb], 2048), "dqs0" if sb == 0 else "dqs1")

        if "q_spill" in taps:
            p.wait("sync", "dqs0", p.cnt["dqs0"])
            p.wait("sync", "dqs1", p.cnt["dqs1"])
            p.dma("sync", tap_out["q_spill"], q_spill, "dtap")

        p.wait("gpsimd", "tn", p.cnt["tn"])

        for k in range(4):
            p.dma("gpsimd", B(AT_KVBN + 2048 * k, 2048),
                  kvbn[128 * k:128 * (k + 1), :], "dwa")
        for k in range(4):
            p.dma("gpsimd", B(AT_KVBV + 2048 * k, 2048),
                  kvbv[128 * k:128 * (k + 1), :], "dwa")
        p.wait("vector", "dwa", p.cnt["dwa"])
        for k in range(4):
            p.op("vector", "tensor_scalar_mul", B(AT_KVBN + 2048 * k, 2048),
                 B(AT_KVBN + 2048 * k, 2048), AF[:, F_KVLN + k:F_KVLN + k + 1])
        for k in range(4):
            p.op("vector", "tensor_scalar_mul", B(AT_KVBV + 2048 * k, 2048),
                 B(AT_KVBV + 2048 * k, 2048), AF[:, F_KVLN + k:F_KVLN + k + 1])
        p.op("vector", "memset", B(O_ONE, 1), 1.0)
        for sl in range(2):
            p.op("vector", "memset", B(AT_QR + 2048 * sl, 2048, 64, 128), 0.0)
            p.op("vector", "memset", B(AT_KRH + 2048 * sl, 2048, 64, 128), 0.0)
        p.op("vector", "memset", B(O_MB + 8, 128, 0, 1), 1.0)
        p.op("vector", "memset", strided_cols(AB, AT_VH + 128, 16, 132, 1), 1.0)
        v_kvb = p.vpub()
        d_ow = 0
        for k in range(16):
            d_ow = p.dma("gpsimd", B(AT_OW + 2048 * k, 2048),
                         ow[128 * k:128 * (k + 1), :], "dwo")

        dq = [0, 0]
        p.wait("sync", "dqs0", p.cnt["dqs0"])
        p.wait("sync", "dqs1", p.cnt["dqs1"])
        p.wait("sync", "dkr", p.cnt["dkr"])
        p.dma("sync", B(AT_QN, 2048), q_spill[0:128, :], "dp0")
        p.dma("sync", B(AT_QR, 2048, 0, 64), q_spill[128:192, :], "dp0")
        dq[0] = p.dma("sync", B(AT_KRH, 2048, 0, 64), kr_spill[0:64, :], "dp0")

        p.wait("tensor", "vc", v_kvb)
        att_dma = 0
        ebuf_free = [0, 0]
        for h in range(NH):
            bufq = h % 2
            qn = AT_QN + 2048 * bufq
            qr = AT_QR + 2048 * bufq
            krh = AT_KRH + 2048 * bufq
            p.wait("tensor", "dp0" if bufq == 0 else "dp1", dq[bufq])
            p.guard_banks(list(range(8)))
            v_kn = 0
            for n in range(4):
                bk = n % 2
                p.guard_banks([bk])
                for kk in range(4):
                    p.op("tensor", "matmul", PB(bk), B(AT_KVBN + 2048 * kk + 128 * h, 128),
                         B(O_KVLAT + 2048 * kk + 512 * n, 512),
                         start=(kk == 0), stop=(kk == 3), skip_group_check=True)
                tv = p.tpub()
                p.wait("vector", "tn", tv)
                p.op("vector", "tensor_copy", B(AT_KNH + 512 * n, 512), PB(bk))
                v_kn = p.vpub()
                p.bank_rd_vc[bk] = v_kn
            v_vh = 0
            for tg in range(4):
                bk = tg % 2
                p.guard_banks([bk])
                for tc in range(4):
                    tt = 4 * tg + tc
                    for kk in range(4):
                        p.op("tensor", "matmul", PB(bk, 128 * tc, 128),
                             B(O_KVLAT + 2048 * kk + 128 * tt, 128),
                             B(AT_KVBV + 2048 * kk + 128 * h, 128),
                             start=(tc == 0 and kk == 0), stop=(tc == 3 and kk == 3),
                             skip_group_check=True)
                tv = p.tpub()
                p.wait("vector", "tn", tv)
                p.op("vector", "tensor_copy",
                     strided_cols(AB, AT_VH + 132 * 4 * tg, 4, 132, 128),
                     PB(bk, 0, 512))
                v_vh = p.vpub()
                p.bank_rd_vc[bk] = v_vh
            if h + 1 < NH:
                hn = h + 1
                nb = hn % 2
                p.wait("sync", "tn", p.cnt["tn"])
                psem = "dp0" if nb == 0 else "dp1"
                p.dma("sync", B(AT_QN + 2048 * nb, 2048),
                      q_spill[192 * hn:192 * hn + 128, :], psem)
                p.dma("sync", B(AT_QR + 2048 * nb, 2048, 0, 64),
                      q_spill[192 * hn + 128:192 * hn + 192, :], psem)
                dq[nb] = p.dma("sync", B(AT_KRH + 2048 * nb, 2048, 0, 64),
                               kr_spill[64 * hn:64 * hn + 64, :], psem)
            sqb = AT_PB
            sqb2 = AT_PB + 2048
            p.wait("vector", "dp0" if bufq == 0 else "dp1", dq[bufq])
            p.wait("vector", "tn", max(ebuf_free))

            def bound_half(nope_off, rope_off, out_slot):
                p.op("vector", "tensor_tensor", B(sqb, 2048), B(nope_off, 2048),
                     B(nope_off, 2048), ALU.mult)
                p.op("vector", "tensor_tensor", B(sqb2, 2048), B(rope_off, 2048),
                     B(rope_off, 2048), ALU.mult)
                v_sq = p.vpub()
                p.wait("tensor", "vc", v_sq)
                for n in range(4):
                    bk = n % 2
                    p.guard_banks([bk])
                    p.op("tensor", "matmul", PS[0:1, 512 * bk:512 * bk + 512],
                         B(O_ONE, 1), B(sqb + 512 * n, 512),
                         start=True, stop=False, skip_group_check=True)
                    p.op("tensor", "matmul", PS[0:1, 512 * bk:512 * bk + 512],
                         B(O_ONE, 1), B(sqb2 + 512 * n, 512),
                         start=False, stop=True, skip_group_check=True)
                    tvq = p.tpub()
                    p.wait("vector", "tn", tvq)
                    p.op("vector", "tensor_reduce",
                         AF[0:1, F_ESB + n:F_ESB + n + 1],
                         PS[0:1, 512 * bk:512 * bk + 512], AX, ALU.max)
                    p.bank_rd_vc[bk] = p.vpub()
                p.vdrain()
                p.op("vector", "tensor_max", AF[0:1, F_ESB + 4:F_ESB + 5],
                     AF[0:1, F_ESB:F_ESB + 1], AF[0:1, F_ESB + 1:F_ESB + 2])
                p.vdrain()
                p.op("vector", "tensor_max", AF[0:1, F_ESB + 5:F_ESB + 6],
                     AF[0:1, F_ESB + 4:F_ESB + 5], AF[0:1, F_ESB + 2:F_ESB + 3])
                p.vdrain()
                p.op("vector", "tensor_max", out_slot,
                     AF[0:1, F_ESB + 5:F_ESB + 6], AF[0:1, F_ESB + 3:F_ESB + 4])
                p.vdrain()

            bound_half(qn, qr, AF[0:1, F_ST + S_MX:F_ST + S_MX + 1])
            bound_half(AT_KNH, krh, AF[0:1, F_ST + S_Z:F_ST + S_Z + 1])
            v_km = p.vpub()
            t_bnd = p.cnt["tn"]
            ebuf_free[0] = max(ebuf_free[0], t_bnd)
            ebuf_free[1] = max(ebuf_free[1], t_bnd)
            p.wait("scalar", "vc", v_km)
            p.op("scalar", "activation", AF[0:1, F_ST + S_T0:F_ST + S_T0 + 1],
                 AF[0:1, F_ST + S_MX:F_ST + S_MX + 1], ACT.Ln, bias=0.0, scale=1.0)
            p.op("scalar", "activation", AF[0:1, F_ST + S_T1:F_ST + S_T1 + 1],
                 AF[0:1, F_ST + S_Z:F_ST + S_Z + 1], ACT.Ln, bias=0.0, scale=1.0)
            s_ln = p.spub()
            p.wait("vector", "sa", s_ln)
            p.op("vector", "tensor_add", AF[0:1, F_ST + S_B2:F_ST + S_B2 + 1],
                 AF[0:1, F_ST + S_T0:F_ST + S_T0 + 1],
                 AF[0:1, F_ST + S_T1:F_ST + S_T1 + 1])
            v_b = p.vpub()
            p.wait("scalar", "vc", v_b)
            p.op("scalar", "activation", AF[0:1, F_ST + S_REC:F_ST + S_REC + 1],
                 AF[0:1, F_ST + S_B2:F_ST + S_B2 + 1], ACT.Exp, bias=0.0, scale=0.5)
            s_mh = p.spub()
            p.wait("vector", "sa", s_mh)
            p.op("vector", "tensor_scalar_mul", B(O_MB, 1, 0, 1),
                 AF[0:1, F_ST + S_REC:F_ST + S_REC + 1], -SCALE)
            v_mb = p.vpub()
            p.wait("tensor", "vc", v_mb)
            p.guard_banks([0])
            p.op("tensor", "matmul", PB(0, 0, 1), B(O_MB + 8, 128, 0, 1),
                 B(O_MB, 1, 0, 1), start=True, stop=True, skip_group_check=True)
            t_bc = p.tpub()
            p.wait("vector", "tn", t_bc)
            p.op("vector", "tensor_copy", ST(S_NEGMS), PB(0, 0, 1))
            v_bias = p.vpub()
            p.bank_rd_vc[0] = v_bias
            p.wait("scalar", "vc", v_bias)
            def emit_pv(j, eb, qg0, s_exp, v_mask):
                for i in range(j, NTC):
                    if i == j:
                        p.wait("tensor", "sa", s_exp[i // 4])
                        p.wait("tensor", "vc", v_mask)
                    elif i % 4 == 0 and i // 4 > qg0:
                        p.wait("tensor", "sa", s_exp[i // 4])
                    bkk = 2 + i // 3
                    off = 129 * (i % 3)
                    p.op("tensor", "matmul",
                         PS[:, 512 * bkk + off: 512 * bkk + off + 129],
                         B(eb + 128 * i, 128), B(AT_VH + 132 * j, 129),
                         start=(j == 0 and i % 3 == 0), stop=False,
                         skip_group_check=True)
                ebuf_free[j % 2] = p.tpub()

            p.wait("tensor", "vc", v_vh)
            pv_args = None
            for j in range(NTC):
                eb = AT_PB + 2048 * (j % 2)
                qg0 = (128 * j) // 512
                s_exp = [0] * 4
                v_mask = 0
                for qg in range(qg0, 4):
                    bk = qg % 2
                    p.guard_banks([bk])
                    p.op("tensor", "matmul", PB(bk), B(AT_KNH + 128 * j, 128),
                         B(qn + 512 * qg, 512), start=True, stop=False,
                         skip_group_check=True)
                    p.op("tensor", "matmul", PB(bk), B(krh + 128 * j, 128),
                         B(qr + 512 * qg, 512), start=False, stop=True,
                         skip_group_check=True)
                    t_st = p.tpub()
                    p.wait("scalar", "tn", max(t_st, ebuf_free[j % 2]))
                    p.op("scalar", "activation", B(eb + 512 * qg, 512), PB(bk),
                         ACT.Exp, bias=ST(S_NEGMS), scale=SCALE)
                    s_exp[qg] = p.spub()
                    p.bank_rd_sa[bk] = s_exp[qg]
                    if qg == qg0:
                        dcol = eb + 128 * j
                        p.wait("vector", "sa", s_exp[qg])
                        p.op("vector", "tensor_tensor", B(dcol, 128), B(dcol, 128),
                             B(O_T01, 128), ALU.mult)
                        v_mask = p.vpub()
                if pv_args is not None:
                    emit_pv(*pv_args)
                pv_args = (j, eb, qg0, list(s_exp), v_mask)
            emit_pv(*pv_args)
            t_pv = ebuf_free[(NTC - 1) % 2]
            p.wait("vector", "tn", t_pv)
            p.wait("vector", "dat", att_dma)
            for b in range(6):
                cnt = 3 if b < 5 else 1
                srcz = PS[:, 512 * (2 + b): 512 * (2 + b) + 129 * cnt].rearrange(
                    "p (t s) -> p t s", s=129)[:, :, 128:129]
                dstz = AF[:, F_ESB + 3 * b: F_ESB + 3 * b + cnt].rearrange(
                    "p (t s) -> p t s", s=1)
                p.op("vector", "tensor_copy", dstz, srcz)
            v_zg = p.vpub()
            p.wait("scalar", "vc", v_zg)
            p.op("scalar", "activation", F(F_MASKF, 16), F(F_ESB, 16), ACT.Ln,
                 bias=0.0, scale=1.0)
            sv = p.spub()
            p.wait("scalar", "sa", sv)
            p.op("scalar", "activation", F(F_SC2, 16), F(F_MASKF, 16), ACT.Exp,
                 bias=0.0, scale=-1.0)
            s_rz = p.spub()
            p.wait("vector", "sa", s_rz)
            for i in range(NTC):
                bkk = 2 + i // 3
                off = 129 * (i % 3)
                p.op("vector", "tensor_scalar_mul", B(AT_ATS2 + 128 * i, 128),
                     PS[:, 512 * bkk + off: 512 * bkk + off + 128],
                     AF[:, F_SC2 + i: F_SC2 + i + 1])
            v_at = p.vpub()
            for bq in range(2, 8):
                p.bank_rd_vc[bq] = v_at
            p.wait("sync", "vc", v_at)
            dst = attn_spill[0:T, 128 * h:128 * (h + 1)].rearrange(
                "(i p) v -> p i v", p=128)
            src2 = B(AT_ATS2, 2048).rearrange("p (i v) -> p i v", v=128)
            att_dma = p.dma("sync", dst, src2, "dat")

        if "attn" in taps:
            p.wait("sync", "dat", p.cnt["dat"])
            p.dma("sync", tap_out["attn"], attn_spill, "dtap")

        p.wait("tensor", "dwo", d_ow)
        p.guard_banks(list(range(8)))
        dab = [0, 0]
        p.wait("sync", "dat", att_dma)
        p.dma("sync", B(AT_ABR, 2048), attn_spill[0:128, :], "db0")
        dab[0] = p.dma("sync", F(F_XB, 2048), x[0:128, :], "db0")
        v_add = 0
        d_x2 = 0
        for c in range(NTC):
            sb = c % 2
            p.wait("tensor", "db0" if sb == 0 else "db1", dab[sb])

            def abf_dst(i0, cnt, sb=sb):
                return B(AT_ABF + 2048 * sb + 128 * i0, 128 * cnt)
            v_abf = transpose_tiles(AT_ABR + 2048 * sb, 16, abf_dst, bA=5, bB=7)
            p.wait("tensor", "vc", v_abf)
            p.guard_banks([0, 1, 2, 3])
            for n in range(4):
                for k in range(16):
                    p.op("tensor", "matmul", PB(n),
                         B(AT_ABF + 2048 * sb + 128 * k, 128),
                         B(AT_OW + 2048 * k + 512 * n, 512),
                         start=(k == 0), stop=(k == 15), skip_group_check=True)
            t_o = p.tpub()
            if c + 1 < NTC:
                nb = (c + 1) % 2
                p.wait("sync", "tn", t_o)
                p.dma("sync", B(AT_ABR + 2048 * nb, 2048),
                      attn_spill[128 * (c + 1):128 * (c + 2), :],
                      "db0" if nb == 0 else "db1")
            p.wait("vector", "tn", t_o)
            p.wait("vector", "dxw", d_x2)
            for n in range(4):
                p.op("vector", "tensor_add", AF[:, F_X2S + 512 * n:F_X2S + 512 * n + 512],
                     PB(n), AF[:, F_XB + 512 * n:F_XB + 512 * n + 512])
            v_add = p.vpub()
            for n in range(4):
                p.bank_rd_vc[n] = v_add
            p.wait("sync", "vc", v_add)
            d_x2 = p.dma("sync", x2_spill[128 * c:128 * (c + 1), :], F(F_X2S, 2048), "dxw")
            if c + 1 < NTC:
                p.wait("sync", "vc", v_add)
                dab[(c + 1) % 2] = p.dma("sync", F(F_XB, 2048),
                                         x[128 * (c + 1):128 * (c + 2), :],
                                         "db0" if (c + 1) % 2 == 0 else "db1")
        if "x2" in taps:
            p.wait("sync", "dxw", p.cnt["dxw"])
            p.dma("sync", tap_out["x2"], x2_spill, "dtap")

        d_x2w = p.cnt["dxw"]
        p.wait("gpsimd", "tn", p.cnt["tn"])
        for k in range(16):
            p.dma("gpsimd", B(C_UA + 512 * k, 512), ua[128 * k:128 * (k + 1), :], "dwa")
        for k in range(4):
            p.dma("gpsimd", B(C_VA + 2048 * k, 2048), va[128 * k:128 * (k + 1), :], "dwa")
        for k in range(16):
            p.dma("gpsimd", B(C_WR + 32 * k, 32), wrt[128 * k:128 * (k + 1), :], "dwa")
        d_cw = p.cnt["dwa"]
        p.wait("sync", "dxw", d_x2w)
        dx2 = [0] * NTC
        d_outs = [0] * NTC
        t_l2tr = [0] * NTC
        t_hus = [0] * NTC

        def xslot(c):
            return F_XB if c % 2 == 0 else F_X2S

        def emit_lnhu(c):
            p.wait("vector", "dc0c" if c % 2 == 0 else "dc1c", dx2[c])
            if c >= 1:
                p.wait("vector", "tn", t_l2tr[c - 1])
            v_ln2 = ln_chain(AF[:, xslot(c):xslot(c) + 2048], B(C_LN2T, 2048), 1e-5)
            p.wait("tensor", "vc", v_ln2)

            def ln2_dst(i0, cnt):
                return B(C_LN2F + 128 * i0, 128 * cnt)
            v_l2f = transpose_tiles(C_LN2T, 16, ln2_dst, bA=5, bB=7)
            t_l2tr[c] = p.cnt["tn"]
            p.wait("tensor", "vc", v_l2f)
            p.guard_banks([4, 6])
            for k in range(16):
                lhs = B(C_LN2F + 128 * k, 128)
                p.op("tensor", "matmul", PB(4), lhs, B(C_UA + 512 * k, 512),
                     start=(k == 0), stop=(k == 15), skip_group_check=True)
                p.op("tensor", "matmul", PB(6, 0, 32), lhs, B(C_WR + 32 * k, 32),
                     start=(k == 0), stop=(k == 15), skip_group_check=True)
            t_hus[c] = p.tpub()

        dx2[0] = p.dma("sync", F(F_XB, 2048), x2_spill[0:128, :], "dc0c")
        p.wait("tensor", "dwa", d_cw)
        emit_lnhu(0)
        for c in range(NTC):
            p.wait("vector", "tn", t_hus[c])
            scp = PB(6, 0, 32)
            p.op("vector", "tensor_reduce", ST(S_M1), scp, AX, ALU.max)
            p.vdrain()
            p.op("vector", "tensor_scalar_mul", ST(S_NEGM1), ST(S_M1), -1.0)
            p.op("vector", "tensor_single_scalar", F(F_MASKF, 32), scp, ST(S_M1),
                 ALU.is_equal)
            p.vdrain()
            p.op("vector", "scalar_tensor_tensor", F(F_SC2, 32), F(F_MASKF, 32),
                 MASKNEG, scp, ALU.mult, ALU.add)
            p.vdrain()
            p.op("vector", "tensor_reduce", ST(S_M2), F(F_SC2, 32), AX, ALU.max)
            v_r1 = p.vpub()
            p.wait("scalar", "vc", v_r1)
            p.op("scalar", "activation", F(F_ESB, 32), scp, ACT.Exp,
                 bias=ST(S_NEGM1), scale=1.0, accum_out=ST(S_ZR))
            s_e = p.spub()
            p.bank_rd_sa[6] = s_e
            p.wait("vector", "sa", s_e)
            p.op("vector", "tensor_single_scalar", F(F_MASKF, 32), scp, ST(S_M2),
                 ALU.is_ge)
            p.vdrain()
            p.op("vector", "scalar_tensor_tensor", F(F_MASKF, 32), F(F_ESB, 32), 1.0,
                 F(F_MASKF, 32), ALU.mult, ALU.mult, accum_out=ST(S_S2G))
            p.vdrain()
            p.op("vector", "tensor_scalar", ST(S_DEN), ST(S_ZR), 1e-8, ST(S_S2G),
                 ALU.mult, ALU.add)
            v_den = p.vpub()
            p.wait("scalar", "vc", v_den)
            p.op("scalar", "activation", ST(S_T0), ST(S_DEN), ACT.Ln, bias=0.0,
                 scale=1.0)
            sv2 = p.spub()
            p.wait("scalar", "sa", sv2)
            p.op("scalar", "activation", ST(S_REC), ST(S_T0), ACT.Exp, bias=0.0,
                 scale=-1.0)
            s_rec = p.spub()
            p.wait("vector", "sa", s_rec)
            p.op("vector", "tensor_scalar_mul", B(C_G, 32), F(F_MASKF, 32), ST(S_REC))
            v_g = p.vpub()
            p.bank_rd_vc[6] = v_g
            p.wait("tensor", "vc", v_g)
            p.guard_banks([7])
            p.op("tensor", "matmul", PB(7, 0, 128, 0, NE), B(C_G, 32), B(O_ID, 128),
                 start=True, stop=True, skip_group_check=True)
            t_g = p.tpub()
            p.wait("vector", "tn", t_g)
            p.op("vector", "tensor_copy", B(C_GF, 128, 0, NE), PB(7, 0, 128, 0, NE))
            v_gf = p.vpub()
            p.bank_rd_vc[7] = v_gf
            p.wait("tensor", "vc", v_gf)
            p.guard_banks([5])
            p.op("tensor", "matmul", PB(5), B(C_GF, 128, 0, NE), B(O_EP, 512, 0, NE),
                 start=True, stop=True, skip_group_check=True)
            p.op("tensor", "matmul", PB(7, 128, 1), B(C_GF, 128, 0, NE),
                 B(O_AW, 1, 0, NE), start=True, stop=True, skip_group_check=True)
            t_ge = p.tpub()
            p.wait("vector", "tn", t_ge)
            p.op("vector", "tensor_copy", B(C_GE, 512), PB(5))
            p.vdrain()
            p.op("vector", "tensor_tensor", B(C_ZSB, 512), PB(4), B(C_GE, 512), ALU.mult)
            p.vdrain()
            p.op("vector", "tensor_scalar", ST(S_F), PB(7, 128, 1), 0.2, 0.9,
                 ALU.mult, ALU.add)
            v_z = p.vpub()
            p.bank_rd_vc[4] = v_z
            p.bank_rd_vc[5] = v_z
            p.bank_rd_vc[7] = v_z
            p.wait("tensor", "vc", v_z)
            p.guard_banks([6])
            for kz in range(4):
                p.op("tensor", "matmul", PB(6, 128 * kz, 128), B(C_ZSB + 128 * kz, 128),
                     B(O_ID, 128), start=True, stop=True, skip_group_check=True)
            t_zt = p.tpub()
            p.wait("vector", "tn", t_zt)
            p.op("vector", "tensor_copy", B(C_ZF, 512), PB(6))
            v_zf = p.vpub()
            p.bank_rd_vc[6] = v_zf
            if c + 1 < NTC:
                if c >= 1:
                    p.wait("sync", "dout", d_outs[c - 1])
                dx2[c + 1] = p.dma("sync", AF[:, xslot(c + 1):xslot(c + 1) + 2048],
                                   x2_spill[128 * (c + 1):128 * (c + 2), :],
                                   "dc0c" if (c + 1) % 2 == 0 else "dc1c")
                emit_lnhu(c + 1)
            p.wait("tensor", "vc", v_zf)
            p.guard_banks([0, 1, 2, 3])
            for n in range(4):
                for k in range(4):
                    p.op("tensor", "matmul", PB(n), B(C_ZF + 128 * k, 128),
                         B(C_VA + 2048 * k + 512 * n, 512),
                         start=(k == 0), stop=(k == 3), skip_group_check=True)
            t_f = p.tpub()
            p.wait("vector", "tn", t_f)
            for n in range(4):
                sl = xslot(c) + 512 * n
                p.op("vector", "scalar_tensor_tensor",
                     AF[:, sl:sl + 512], PB(n), ST(S_F),
                     AF[:, sl:sl + 512], ALU.mult, ALU.add)
            v_out = p.vpub()
            for n in range(4):
                p.bank_rd_vc[n] = v_out
            p.wait("sync", "vc", v_out)
            d_outs[c] = p.dma("sync", out[128 * c:128 * (c + 1), :],
                              AF[:, xslot(c):xslot(c) + 2048], "dout")

        for s in p.dma_sems:
            p.wait("sync", s, p.cnt[s])

        with nc.Block() as block:
            @block.sync
            def _(e):
                p.replay("sync", e)

            @block.gpsimd
            def _(e):
                p.replay("gpsimd", e)

            @block.tensor
            def _(e):
                p.replay("tensor", e)

            @block.vector
            def _(e):
                p.replay("vector", e)

            @block.scalar
            def _(e):
                p.replay("scalar", e)

    return nc


def prep_inputs(inputs):
    f32 = np.float32
    qa = np.ascontiguousarray(inputs["L0_attn_q_a_w"], dtype=f32)
    kva = np.ascontiguousarray(inputs["L0_attn_kv_a_w"], dtype=f32)
    qb = np.ascontiguousarray(inputs["L0_attn_q_b_w"], dtype=f32)
    kvb = np.asarray(inputs["L0_attn_kv_b_w"], dtype=f32)
    kvbn = np.ascontiguousarray(kvb.reshape(KL, NH, DN + DV)[:, :, :DN].reshape(KL, NH * DN))
    kvbv = np.ascontiguousarray(kvb.reshape(KL, NH, DN + DV)[:, :, DN:].reshape(KL, NH * DV))
    ow = np.ascontiguousarray(inputs["L0_attn_o_w"], dtype=f32)
    wrt = np.ascontiguousarray(np.asarray(inputs["L0_ffn_Wr"], dtype=f32).T)
    ua = np.ascontiguousarray(np.asarray(inputs["L0_ffn_U"], dtype=f32).transpose(1, 0, 2).reshape(D, NE * RK))
    va = np.ascontiguousarray(np.asarray(inputs["L0_ffn_V"], dtype=f32).reshape(NE * RK, D))
    attrt = np.ascontiguousarray(np.asarray(inputs["L0_ffn_attr"], dtype=f32).T)
    wac = np.ascontiguousarray(np.asarray(inputs["L0_ffn_Wa"], dtype=f32)[0][:, None])
    qln = np.ascontiguousarray(np.asarray(inputs["L0_attn_q_a_ln_w"], dtype=f32).reshape(12, 128).T)
    kvln = np.ascontiguousarray(np.asarray(inputs["L0_attn_kv_a_ln_w"], dtype=f32).reshape(4, 128).T)
    return dict(qa=qa, kva=kva, qb=qb, kvbn=kvbn, kvbv=kvbv, ow=ow, wrt=wrt,
                ua=ua, va=va, attrt=attrt, wac=wac, qln=qln, kvln=kvln)


def kernel(**inputs):
    shared = prep_inputs(inputs)
    x = np.ascontiguousarray(np.asarray(inputs["L0_attn_hidden_states"], dtype=np.float32))
    nc = build()
    in_maps = []
    for c in range(8):
        m = dict(shared)
        m["x"] = np.ascontiguousarray(x[c])
        in_maps.append(m)
    res = run_bass_kernel_spmd(nc, in_maps, core_ids=list(range(8)))
    return np.stack([res.results[c]["out"] for c in range(8)], axis=0)
